# version 02
# baseline (speedup 1.0000x reference)
import contextlib
import numpy as np
import ml_dtypes
import concourse.bass as bass
import concourse.mybir as mybir
from concourse.bass_utils import run_bass_kernel_spmd

F32 = mybir.dt.float32
BF16 = mybir.dt.bfloat16
I32 = mybir.dt.int32
AF = mybir.ActivationFunctionType
ALU = mybir.AluOpType
AX = mybir.AxisListType

T = 2048
D = 4096
NCORES = 8
QSCALE = 128.0 ** -0.5
AWQ_ORDER = [0, 4, 1, 5, 2, 6, 3, 7]

bf = ml_dtypes.bfloat16



def build_l0_pre():
    nc = bass.Bass("TRN2", target_bir_lowering=False)
    x = nc.dram_tensor("x", [T, D], F32, kind="ExternalInput").ap()
    wqT = nc.dram_tensor("wqT", [D, D], F32, kind="ExternalInput").ap()
    wkT = nc.dram_tensor("wkT", [D, 1024], F32, kind="ExternalInput").ap()
    wvT = nc.dram_tensor("wvT", [D, 1024], F32, kind="ExternalInput").ap()
    bv = nc.dram_tensor("bv", [128, 8], F32, kind="ExternalInput").ap()
    xT = nc.dram_tensor("xT", [D, T], F32, kind="ExternalOutput").ap()
    xn = nc.dram_tensor("xn", [D, T], BF16, kind="ExternalOutput").ap()
    q0 = nc.dram_tensor("q0", [D, T], BF16, kind="ExternalOutput").ap()
    k0 = nc.dram_tensor("k0", [1024, T], BF16, kind="ExternalOutput").ap()
    v0 = nc.dram_tensor("v0", [1024, T], BF16, kind="ExternalOutput").ap()

    with contextlib.ExitStack() as top:
        xn_sb = top.enter_context(nc.sbuf_tensor("xn_sb", [128, 32, T], BF16))
        identB = top.enter_context(nc.sbuf_tensor("identB", [128, 128], BF16))
        identF = top.enter_context(nc.sbuf_tensor("identF", [128, 128], F32))
        cpool = top.enter_context(nc.sbuf_tensor("cpool", [128, 2], F32))
        block = top.enter_context(nc.Block())
        sem = {n: top.enter_context(nc.semaphore(n)) for n in
               ["ld0", "ld1", "sq", "var", "srt", "nb", "xnr", "tb", "tf", "eb", "ef",
                "xto", "xno", "idn", "bvl", "wld0", "wld1", "mmq", "evq",
                "qout0", "qout1", "vss", "sss"]}
        vss = [0]
        sss = [0]

        def vbar(v, inst):
            vss[0] += 1
            inst.then_inc(sem["vss"], 1)
            v.wait_ge(sem["vss"], vss[0])

        def sbar(sc, inst):
            sss[0] += 1
            inst.then_inc(sem["sss"], 1)
            sc.wait_ge(sem["sss"], sss[0])

        @block.gpsimd
        def _(g):
            g.memset(identB[:], 1.0).then_inc(sem["idn"], 1)
            g.memset(identF[:], 1.0).then_inc(sem["idn"], 1)
            g.wait_ge(sem["idn"], 2)
            g.affine_select(out=identB[:], in_=identB[:], compare_op=ALU.is_equal,
                            fill=0.0, base=0, pattern=[[-1, 128]], channel_multiplier=1)
            g.affine_select(out=identF[:], in_=identF[:], compare_op=ALU.is_equal,
                            fill=0.0, base=0, pattern=[[-1, 128]],
                            channel_multiplier=1)
            g.memset(cpool[:, 0:1], QSCALE)
            g.memset(cpool[:, 1:2], -0.5).then_inc(sem["idn"], 1)

        NT = 16
        with contextlib.ExitStack() as st0:
            x_tok = [st0.enter_context(nc.sbuf_tensor(f"x_tok{i}", [128, D], F32)) for i in range(2)]
            xn_tok = [st0.enter_context(nc.sbuf_tensor(f"xn_tok{i}", [128, D], BF16)) for i in range(2)]
            xsq = st0.enter_context(nc.sbuf_tensor("xsq", [128, D], BF16))
            stat = [st0.enter_context(nc.sbuf_tensor(f"stat{i}", [128, 8], F32)) for i in range(2)]
            xT_stage = st0.enter_context(nc.sbuf_tensor("xT_st", [128, 32, 128], F32))
            psB = [st0.enter_context(nc.psum_tensor(f"psB{i}", [128, 8, 128], BF16)) for i in range(2)]
            psF = [st0.enter_context(nc.psum_tensor(f"psF{i}", [128, 4, 128], F32)) for i in range(2)]

            @block.sync
            def _(s):
                for i in range(NT):
                    b = i % 2
                    if i >= 2:
                        s.wait_ge(sem["tf"], (i - 1) * 8)
                    s.dma_start(out=x_tok[b][:], in_=x[bass.ts(i, 128), :]).then_inc(sem[f"ld{b}"], 16)

            @block.vector
            def _(v):
                for i in range(NT):
                    b = i % 2
                    v.wait_ge(sem[f"ld{b}"], 16 * (i // 2 + 1))
                    if i >= 2:
                        v.wait_ge(sem["srt"], i - 1)
                    vbar(v, v.reduce_sum(stat[b][:, 0:1], x_tok[b][:], axis=AX.X))
                    v.wait_ge(sem["sq"], i + 1)
                    vbar(v, v.reduce_sum(stat[b][:, 1:2], xsq[:], axis=AX.X))
                    vbar(v, v.tensor_scalar_mul(stat[b][:, 2:3], stat[b][:, 0:1], 1.0 / D))
                    vbar(v, v.tensor_scalar(out=stat[b][:, 7:8], in0=stat[b][:, 2:3],
                                            scalar1=stat[b][:, 2:3], scalar2=-1e-5,
                                            op0=ALU.mult, op1=ALU.add))
                    v.scalar_tensor_tensor(out=stat[b][:, 3:4], in0=stat[b][:, 1:2],
                                           scalar=1.0 / D, in1=stat[b][:, 7:8],
                                           op0=ALU.mult, op1=ALU.subtract).then_inc(sem["var"], 1)
                    v.wait_ge(sem["srt"], i + 1)
                    vbar(v, v.scalar_tensor_tensor(out=stat[b][:, 6:7], in0=stat[b][:, 2:3],
                                                   scalar=-1.0, in1=stat[b][:, 5:6],
                                                   op0=ALU.mult, op1=ALU.mult))
                    v.tensor_scalar(out=xn_tok[b][:], in0=x_tok[b][:],
                                    scalar1=stat[b][:, 5:6], scalar2=stat[b][:, 6:7],
                                    op0=ALU.mult, op1=ALU.add).then_inc(sem["xnr"], 1)
                    for gi in range(8):
                        v.wait_ge(sem["tf"], i * 8 + gi + 1)
                        if gi == 0 and i >= 1:
                            v.wait_ge(sem["xto"], 16 * i)
                        v.tensor_copy(xT_stage[:, bass.ts(gi, 4), :],
                                      psF[gi % 2][:]).then_inc(sem["ef"], 1)

            @block.scalar
            def _(sc):
                for i in range(NT):
                    b = i % 2
                    sc.wait_ge(sem[f"ld{b}"], 16 * (i // 2 + 1))
                    if i >= 1:
                        sc.wait_ge(sem["var"], i)
                    sc.activation(out=xsq[:], in_=x_tok[b][:], func=AF.Square).then_inc(sem["sq"], 1)
                    sc.wait_ge(sem["var"], i + 1)
                    sc.wait_ge(sem["idn"], 3)
                    sbar(sc, sc.activation(out=stat[b][:, 4:5], in_=stat[b][:, 3:4], func=AF.Ln))
                    sc.activation(out=stat[b][:, 5:6], in_=stat[b][:, 4:5], func=AF.Exp,
                                  scale=cpool[:, 1:2]).then_inc(sem["srt"], 1)
                    for gi in range(4):
                        sc.wait_ge(sem["tb"], i * 4 + gi + 1)
                        sc.copy(xn_sb[:, bass.ts(gi, 8), bass.ts(i, 128)],
                                psB[gi % 2][:]).then_inc(sem["eb"], 1)

            @block.tensor
            def _(t):
                t.wait_ge(sem["idn"], 3)
                for i in range(NT):
                    b = i % 2
                    t.wait_ge(sem["xnr"], i + 1)
                    for gi in range(4):
                        if i * 4 + gi >= 2:
                            t.wait_ge(sem["eb"], i * 4 + gi - 1)
                        for j in range(8):
                            ch = gi * 8 + j
                            ti = t.transpose(psB[gi % 2][:, j, :],
                                             xn_tok[b][:, bass.ts(ch, 128)], identB[:])
                        ti.then_inc(sem["tb"], 1)
                    for gi in range(8):
                        if i * 8 + gi >= 2:
                            t.wait_ge(sem["ef"], i * 8 + gi - 1)
                        for j in range(4):
                            ch = gi * 4 + j
                            ti = t.transpose(psF[gi % 2][:, j, :],
                                             x_tok[b][:, bass.ts(ch, 128)], identF[:])
                        ti.then_inc(sem["tf"], 1)

            @block.gpsimd
            def _(g):
                for i in range(NT):
                    g.wait_ge(sem["ef"], i * 8 + 8)
                    g.dma_start(out=xT.rearrange("(ch p) t -> p ch t", p=128)[:, :, bass.ts(i, 128)],
                                in_=xT_stage[:]).then_inc(sem["xto"], 16)

            @block.sync
            def _(s):
                s.wait_ge(sem["eb"], NT * 4)
                s.dma_start(out=xn.rearrange("(ch p) t -> p ch t", p=128),
                            in_=xn_sb[:]).then_inc(sem["xno"], 16)

        with contextlib.ExitStack() as st1:
            wbuf = [st1.enter_context(nc.sbuf_tensor(f"wbuf{i}", [128, 32, 128], BF16)) for i in range(2)]
            qstage = [st1.enter_context(nc.sbuf_tensor(f"qstage{i}", [128, T], BF16)) for i in range(2)]
            bv_sb = st1.enter_context(nc.sbuf_tensor("bv_sb", [128, 8], F32))
            psQ = [st1.enter_context(nc.psum_tensor(f"psQ{i}", [128, 4, 512], F32)) for i in range(2)]

            jobs = ([(wqT, o, QSCALE, None, q0) for o in range(32)]
                    + [(wkT, o, 1.0, None, k0) for o in range(8)]
                    + [(wvT, o, 1.0, "bv", v0) for o in range(8)])

            @block.sync
            def _(s):
                s.wait_ge(sem["xno"], 16)
                s.wait_ge(sem["xto"], 16 * NT)
                s.dma_start(out=bv_sb[:], in_=bv).then_inc(sem["bvl"], 16)

            @block.gpsimd
            def _(g):
                g.wait_ge(sem["xno"], 16)
                g.wait_ge(sem["xto"], 16 * NT)
                for j, (wT, o, scl, bias, dest) in enumerate(jobs):
                    if j >= 2:
                        g.wait_ge(sem["mmq"], j - 1)
                    g.dma_start(out=wbuf[j % 2][:],
                                in_=wT.rearrange("(ch p) o -> p ch o", p=128)[:, :, bass.ts(o, 128)]
                                ).then_inc(sem[f"wld{j % 2}"], 16)

            @block.tensor
            def _(t):
                t.wait_ge(sem["bvl"], 16)
                for j, (wT, o, scl, bias, dest) in enumerate(jobs):
                    t.wait_ge(sem[f"wld{j % 2}"], 16 * (j // 2 + 1))
                    if j >= 2:
                        t.wait_ge(sem["evq"], (j - 1) * 4)
                    for tc in range(4):
                        for ch in range(32):
                            mi = t.matmul(psQ[j % 2][:, tc, :], wbuf[j % 2][:, ch, :],
                                          xn_sb[:, ch, bass.ts(tc, 512)],
                                          start=(ch == 0), stop=(ch == 31))
                    mi.then_inc(sem["mmq"], 1)

            @block.vector
            def _(v):
                for j, (wT, o, scl, bias, dest) in enumerate(jobs):
                    v.wait_ge(sem["mmq"], j + 1)
                    if j >= 2:
                        v.wait_ge(sem[f"qout{j % 2}"], 16 * (j // 2))
                    for tc in range(4):
                        dst = qstage[j % 2][:, bass.ts(tc, 512)]
                        src = psQ[j % 2][:, tc, :]
                        if bias == "bv":
                            v.tensor_scalar_add(dst, src, bv_sb[:, o:o + 1]).then_inc(sem["evq"], 1)
                        elif scl != 1.0:
                            v.tensor_scalar_mul(dst, src, scl).then_inc(sem["evq"], 1)
                        else:
                            v.tensor_copy(dst, src).then_inc(sem["evq"], 1)

            @block.sync
            def _(s):
                for j, (wT, o, scl, bias, dest) in enumerate(jobs):
                    s.wait_ge(sem["evq"], (j + 1) * 4)
                    s.dma_start(out=dest[bass.ts(o, 128), :],
                                in_=qstage[j % 2][:]).then_inc(sem[f"qout{j % 2}"], 16)
                s.wait_ge(sem["qout0"], 16 * ((len(jobs) + 1) // 2))
                s.wait_ge(sem["qout1"], 16 * (len(jobs) // 2))
                s.wait_ge(sem["xno"], 16)

    return nc



def build_l0_attn(NH=32, dbg_pr=False):
    nc = bass.Bass("TRN2", target_bir_lowering=False)
    q0 = nc.dram_tensor("q0", [D, T], BF16, kind="ExternalInput").ap()
    k0 = nc.dram_tensor("k0", [1024, T], BF16, kind="ExternalInput").ap()
    v0 = nc.dram_tensor("v0", [1024, T], BF16, kind="ExternalInput").ap()
    ctx0 = nc.dram_tensor("ctx0", [D, T], BF16, kind="ExternalOutput").ap()
    pr = nc.dram_tensor("pr", [2, T, T], BF16,
                        kind="ExternalOutput" if dbg_pr else "Internal").ap()

    NKV = 8

    with contextlib.ExitStack() as top:
        K_sb = top.enter_context(nc.sbuf_tensor("K_sb", [128, NKV, T], BF16))
        Vt_sb = top.enter_context(nc.sbuf_tensor("Vt_sb", [128, NKV, 16, 128], BF16))
        qh_sb = [top.enter_context(nc.sbuf_tensor(f"qh{i}", [128, T], BF16)) for i in range(2)]
        probs_sb = [top.enter_context(nc.sbuf_tensor(f"pb{i}", [128, T], BF16)) for i in range(2)]
        pT_sb = [top.enter_context(nc.sbuf_tensor(f"pT{i}", [128, T], BF16)) for i in range(2)]
        ctx_st = top.enter_context(nc.sbuf_tensor("ctx_st", [128, T], BF16))
        mx = [top.enter_context(nc.sbuf_tensor(f"mx{i}", [128, 2], F32)) for i in range(2)]
        sums = [top.enter_context(nc.sbuf_tensor(f"sums{i}", [128, 2], F32)) for i in range(2)]
        ps = [top.enter_context(nc.psum_tensor(f"ps{i}", [128, 4, 512], F32)) for i in range(2)]
        block = top.enter_context(nc.Block())
        sem = {n: top.enter_context(nc.semaphore(n)) for n in
               ["k", "vt", "qh0", "qh1", "sc", "mxs", "ex", "nm",
                "pw0", "pw1", "pt0", "pt1", "cx", "cev", "cout", "vss"]}
        vss = [0]

        def vbar(v, inst):
            vss[0] += 1
            inst.then_inc(sem["vss"], 1)
            v.wait_ge(sem["vss"], vss[0])

        @block.sync
        def _(s):
            s.dma_start(out=K_sb[:], in_=k0.rearrange("(h p) t -> p h t", p=128)).then_inc(sem["k"], 16)
            for kvh in range(NKV):
                for j in range(16):
                    s.dma_start(out=Vt_sb[:, kvh, j, :],
                                in_=v0[bass.ts(kvh, 128), bass.ts(j, 128)],
                                transpose=True).then_inc(sem["vt"], 16)

        @block.gpsimd
        def _(g):
            def ctx_write(hh):
                g.wait_ge(sem["cev"], hh + 1)
                g.dma_start(out=ctx0[bass.ts(hh, 128), :],
                            in_=ctx_st[:]).then_inc(sem["cout"], 16)

            for h in range(NH):
                if h >= 3:
                    ctx_write(h - 3)
                if h >= 2:
                    g.wait_ge(sem["sc"], 16 * (h - 1))
                g.dma_start(out=qh_sb[h % 2][:],
                            in_=q0[bass.ts(h, 128), :]).then_inc(sem[f"qh{h % 2}"], 16)
            for hh in range(max(0, NH - 3), NH):
                ctx_write(hh)
            g.wait_ge(sem["cout"], 16 * NH)

        @block.tensor
        def _(t):
            t.wait_ge(sem["k"], 16)
            t.wait_ge(sem["vt"], 16 * NKV * 16)
            for h in range(NH):
                kvh = h // 4
                t.wait_ge(sem[f"qh{h % 2}"], 16 * (h // 2 + 1))
                for qt in range(16):
                    m = 16 * h + qt
                    if m >= 2:
                        t.wait_ge(sem["ex"], m - 1)
                    if qt == 1 and h >= 1:
                        t.wait_ge(sem["cev"], h)
                    for tc in range(4):
                        mi = t.matmul(ps[m % 2][:, tc, :],
                                      qh_sb[h % 2][:, bass.ts(qt, 128)],
                                      K_sb[:, kvh, bass.ts(tc, 512)],
                                      start=True, stop=True)
                    mi.then_inc(sem["sc"], 1)
                for kc in range(16):
                    n = 16 * h + kc
                    t.wait_ge(sem[f"pt{n % 2}"], 16 * (n // 2 + 1))
                    for tc in range(4):
                        mi = t.matmul(ps[1][:, tc, :],
                                      Vt_sb[:, kvh, kc, :],
                                      pT_sb[n % 2][:, bass.ts(tc, 512)],
                                      start=(kc == 0), stop=(kc == 15))
                    mi.then_inc(sem["cx"], 1)

        @block.vector
        def _(v):
            for h in range(NH):
                for qt in range(16):
                    m = 16 * h + qt
                    b = m % 2
                    v.wait_ge(sem["sc"], m + 1)
                    vbar(v, v.reduce_max(mx[b][:, 0:1], ps[b][:], axis=AX.XY))
                    v.tensor_scalar_mul(mx[b][:, 1:2], mx[b][:, 0:1], -1.0).then_inc(sem["mxs"], 1)
                    v.wait_ge(sem["ex"], m + 1)
                    vbar(v, v.reciprocal(sums[b][:, 1:2], sums[b][:, 0:1]))
                    v.tensor_scalar_mul(probs_sb[b][:], probs_sb[b][:],
                                        sums[b][:, 1:2]).then_inc(sem["nm"], 1)

        @block.scalar
        def _(sc):
            for h in range(NH):
                for qt in range(16):
                    m = 16 * h + qt
                    b = m % 2
                    sc.wait_ge(sem["mxs"], m + 1)
                    if m >= 2:
                        sc.wait_ge(sem[f"pw{m % 2}"], 16 * (m // 2))
                    sc.activation(out=probs_sb[b][:], in_=ps[b][:], func=AF.Exp,
                                  bias=mx[b][:, 1:2],
                                  accum_out=sums[b][:, 0:1]).then_inc(sem["ex"], 1)
                for kc in range(16):
                    n = 16 * h + kc
                    sc.wait_ge(sem["pw0"], 16 * 8 * (h + 1))
                    sc.wait_ge(sem["pw1"], 16 * 8 * (h + 1))
                    if n >= 2:
                        sc.wait_ge(sem["cx"], n - 1)
                    sc.dma_start(out=pT_sb[n % 2][:], in_=pr[h % 2][:, bass.ts(kc, 128)],
                                 transpose=True).then_inc(sem[f"pt{n % 2}"], 16)
                sc.wait_ge(sem["cx"], 16 * (h + 1))
                if h >= 1:
                    sc.wait_ge(sem["cout"], 16 * h)
                sc.copy(ctx_st[:], ps[1][:]).then_inc(sem["cev"], 1)

        @block.sync
        def _(s):
            for h in range(NH):
                for qt in range(16):
                    m = 16 * h + qt
                    s.wait_ge(sem["nm"], m + 1)
                    if qt == 0 and h >= 2:
                        s.wait_ge(sem["cx"], 16 * (h - 1))
                    s.dma_start(out=pr[h % 2, bass.ts(qt, 128), :],
                                in_=probs_sb[m % 2][:]).then_inc(sem[f"pw{m % 2}"], 16)

    return nc



def build_l0_ffn1():
    nc = bass.Bass("TRN2", target_bir_lowering=False)
    xn = nc.dram_tensor("xn", [D, T], BF16, kind="ExternalInput").ap()
    gqT = nc.dram_tensor("gqT", [1408, D], I32, kind="ExternalInput").ap()
    uqT = nc.dram_tensor("uqT", [1408, D], I32, kind="ExternalInput").ap()
    Sg = nc.dram_tensor("Sg", [11, 128, 8, 32], F32, kind="ExternalInput").ap()
    Su = nc.dram_tensor("Su", [11, 128, 8, 32], F32, kind="ExternalInput").ap()
    zgT = nc.dram_tensor("zgT", [1408, 32], I32, kind="ExternalInput").ap()
    zuT = nc.dram_tensor("zuT", [1408, 32], I32, kind="ExternalInput").ap()
    h0 = nc.dram_tensor("h0", [11008, T], BF16, kind="ExternalOutput").ap()
    sigd = nc.dram_tensor("sigd", [32, T], F32, kind="Internal").ap()

    NCH = 11

    with contextlib.ExitStack() as top:
        xn_sb = top.enter_context(nc.sbuf_tensor("xn_sb", [128, 32, T], BF16))
        identB = top.enter_context(nc.sbuf_tensor("identB", [128, 128], BF16))
        ones1 = top.enter_context(nc.sbuf_tensor("ones1", [128, 1], BF16))
        words = {m: top.enter_context(nc.sbuf_tensor(f"words_{m}", [128, D], I32)) for m in "gu"}
        nib = top.enter_context(nc.sbuf_tensor("nib", [128, D // 2], I32))
        wsc = top.enter_context(nc.sbuf_tensor("wsc", [128, D // 2], BF16))
        W_sb = {m: top.enter_context(nc.sbuf_tensor(f"W_{m}", [128, 32, 128], BF16)) for m in "gu"}
        S_sb = {m: top.enter_context(nc.sbuf_tensor(f"S_{m}", [128, 8, 32], F32)) for m in "gu"}
        z_sb = {m: top.enter_context(nc.sbuf_tensor(f"z_{m}", [128, 32], I32)) for m in "gu"}
        znib = top.enter_context(nc.sbuf_tensor("znib", [128, 32], I32))
        nzs = top.enter_context(nc.sbuf_tensor("nzs", [128, 32], F32))
        nzs_bf = top.enter_context(nc.sbuf_tensor("nzs_bf", [128, 32], BF16))
        nzsT = {m: top.enter_context(nc.sbuf_tensor(f"nzsT_{m}", [32, 128], BF16)) for m in "gu"}
        sig_sb = top.enter_context(nc.sbuf_tensor("sig_sb", [32, T], BF16))
        sig_row = top.enter_context(nc.sbuf_tensor("sig_row", [1, 512], F32))
        sg_sb = top.enter_context(nc.sbuf_tensor("sg_sb", [128, 512], BF16))
        h_st = [top.enter_context(nc.sbuf_tensor(f"h_st{i}", [128, 2, 512], BF16)) for i in range(2)]

        psW = [top.enter_context(nc.psum_tensor(f"psW{i}", [128, 8, 128], BF16)) for i in range(2)]
        psZ = top.enter_context(nc.psum_tensor("psZ", [32, 128], BF16))
        psSig = top.enter_context(nc.psum_tensor("psSig", [1, 512], F32))
        gps = [top.enter_context(nc.psum_tensor(f"gps{i}", [128, 512], F32)) for i in range(2)]
        ups = [top.enter_context(nc.psum_tensor(f"ups{i}", [128, 512], F32)) for i in range(2)]

        block = top.enter_context(nc.Block())
        sem = {n: top.enter_context(nc.semaphore(n)) for n in
               ["xnl", "idn", "wldg", "wldu", "zl", "sgm", "sge", "sgw", "sgl",
                "tts", "wtr", "wev", "ztr", "zev",
                "mmg", "mmu", "sil", "hml", "hout0", "hout1", "vss", "sss"]}
        vss = [0]
        sss = [0]

        def vbar(v, inst):
            vss[0] += 1
            inst.then_inc(sem["vss"], 1)
            v.wait_ge(sem["vss"], vss[0])

        slices = [(ch, c) for ch in range(NCH) for c in range(8)]
        WID = {ch: (96 if ch == NCH - 1 else 128) for ch in range(NCH)}

        @block.sync
        def _(s):
            s.dma_start(out=xn_sb[:], in_=xn.rearrange("(ch p) t -> p ch t", p=128)).then_inc(sem["xnl"], 16)

        @block.gpsimd
        def _(g):
            g.memset(identB[:], 1.0).then_inc(sem["idn"], 1)
            g.memset(ones1[:], 1.0).then_inc(sem["idn"], 1)
            g.wait_ge(sem["idn"], 2)
            g.affine_select(out=identB[:], in_=identB[:], compare_op=ALU.is_equal,
                            fill=0.0, base=0, pattern=[[-1, 128]],
                            channel_multiplier=1).then_inc(sem["idn"], 1)

        @block.tensor
        def _(t):
            t.wait_ge(sem["xnl"], 16)
            t.wait_ge(sem["idn"], 3)
            for g in range(32):
                for tc in range(4):
                    if 4 * g + tc >= 1:
                        t.wait_ge(sem["sge"], 4 * g + tc)
                    t.matmul(psSig[:], ones1[:], xn_sb[:, g, bass.ts(tc, 512)],
                             start=True, stop=True).then_inc(sem["sgm"], 1)

        @block.scalar
        def _(sc):
            for g in range(32):
                for tc in range(4):
                    sc.wait_ge(sem["sgm"], 4 * g + tc + 1)
                    if 4 * g + tc >= 1:
                        sc.wait_ge(sem["sgw"], 16 * (4 * g + tc))
                    sc.copy(sig_row[:], psSig[:]).then_inc(sem["sge"], 1)

        @block.sync
        def _(s):
            for g in range(32):
                for tc in range(4):
                    s.wait_ge(sem["sge"], 4 * g + tc + 1)
                    s.dma_start(out=sigd[g:g + 1, bass.ts(tc, 512)],
                                in_=sig_row[:]).then_inc(sem["sgw"], 16)

        @block.gpsimd
        def _(g):
            g.wait_ge(sem["sgw"], 16 * 128)
            g.dma_start(out=sig_sb[:], in_=sigd).then_inc(sem["sgl"], 16)


        @block.sync
        def _(s):
            for ch in range(NCH):
                if ch >= 1:
                    s.wait_ge(sem["zev"], 16 * ch)
                for m, zT, Sd in (("g", zgT, Sg), ("u", zuT, Su)):
                    s.dma_start(out=z_sb[m][:], in_=zT[bass.ts(ch, 128), :]).then_inc(sem["zl"], 16)
                    s.dma_start(out=S_sb[m][:], in_=Sd[ch]).then_inc(sem["zl"], 16)

        def emit_hmul(v, si):
            ch, c = slices[si]
            w = WID[ch]
            for tc in range(4):
                u = 4 * si + tc
                v.wait_ge(sem["sil"], u + 1)
                v.wait_ge(sem["mmu"], u + 1)
                if tc == 0 and si >= 2:
                    v.wait_ge(sem[f"hout{si % 2}"], 32 * (si // 2))
                if tc == 2:
                    v.wait_ge(sem[f"hout{si % 2}"], 32 * (si // 2) + 16)
                v.tensor_mul(h_st[si % 2][0:w, tc % 2, :], sg_sb[0:w, :],
                             ups[tc % 2][0:w, :]).then_inc(sem["hml"], 1)

        @block.vector
        def _(v):
            for si, (ch, c) in enumerate(slices):
                sh = 4 * AWQ_ORDER[c]
                if si >= 1:
                    emit_hmul(v, si - 1)
                for mi, m in enumerate(("g", "u")):
                    k = 2 * si + mi
                    v.wait_ge(sem["wldg" if m == "g" else "wldu"], 16 * (ch + 1))
                    v.wait_ge(sem["zl"], 64 * (ch + 1))
                    for half in range(2):
                        if 2 * k + half >= 1:
                            v.wait_ge(sem["tts"], 2 * k + half)
                        vbar(v, v.tensor_scalar(out=nib[:], in0=words[m][:, bass.ts(half, D // 2)],
                                                scalar1=sh, scalar2=15,
                                                op0=ALU.logical_shift_right,
                                                op1=ALU.bitwise_and))
                        v.wait_ge(sem["wtr"], 4 * k + 2 * half)
                        v.tensor_tensor(wsc[:].rearrange("p (a b) -> p a b", b=128),
                                        nib[:].rearrange("p (a b) -> p a b", b=128),
                                        S_sb[m][:, c, 16 * half:16 * half + 16][:, :, None].broadcast_to([128, 16, 128]),
                                        ALU.mult).then_inc(sem["tts"], 1)
                    vbar(v, v.tensor_scalar(out=znib[:], in0=z_sb[m][:], scalar1=sh,
                                            scalar2=15, op0=ALU.logical_shift_right,
                                            op1=ALU.bitwise_and))
                    vbar(v, v.scalar_tensor_tensor(out=nzs[:], in0=znib[:], scalar=-1.0,
                                                   in1=S_sb[m][:, c, :], op0=ALU.mult,
                                                   op1=ALU.mult))
                    if k >= 1:
                        v.wait_ge(sem["ztr"], k)
                    vz = v.tensor_copy(nzs_bf[:], nzs[:])
                    vz.then_inc(sem["vss"], 1); vss[0] += 1
                    v.wait_ge(sem["vss"], vss[0])
            emit_hmul(v, len(slices) - 1)

        @block.tensor
        def _(t):
            for si, (ch, c) in enumerate(slices):
                w = WID[ch]
                for mi, m in enumerate(("g", "u")):
                    k = 2 * si + mi
                    for half in range(2):
                        t.wait_ge(sem["tts"], 2 * k + half + 1)
                        for gi in range(2):
                            gidx = 4 * k + 2 * half + gi
                            if gidx >= 2:
                                t.wait_ge(sem["wev"], gidx - 1)
                            for j in range(8):
                                gg = gi * 8 + j
                                ti = t.transpose(psW[gidx % 2][:, j, 0:w],
                                                 wsc[0:w, bass.ts(gg, 128)], identB[0:w, 0:w])
                            ti.then_inc(sem["wtr"], 1)
                    t.wait_ge(sem["vss"], 5 * (k + 1))
                    if k >= 1:
                        t.wait_ge(sem["zev"], k)
                    t.transpose(psZ[:], nzs_bf[:], identB[:]).then_inc(sem["ztr"], 1)
                t.wait_ge(sem["wev"], (2 * si + 2) * 4)
                t.wait_ge(sem["zev"], 2 * si + 2)
                t.wait_ge(sem["sgl"], 16)
                for tc in range(4):
                    u = 4 * si + tc
                    if u >= 2:
                        t.wait_ge(sem["sil"], u - 1)
                        t.wait_ge(sem["hml"], u - 1)
                    for g in range(32):
                        t.matmul(gps[tc % 2][0:w, :], W_sb["g"][:, g, 0:w],
                                 xn_sb[:, g, bass.ts(tc, 512)],
                                 start=(g == 0), stop=False)
                    t.matmul(gps[tc % 2][0:w, :], nzsT["g"][:, 0:w],
                             sig_sb[:, bass.ts(tc, 512)],
                             start=False, stop=True).then_inc(sem["mmg"], 1)
                    for g in range(32):
                        t.matmul(ups[tc % 2][0:w, :], W_sb["u"][:, g, 0:w],
                                 xn_sb[:, g, bass.ts(tc, 512)],
                                 start=(g == 0), stop=False)
                    t.matmul(ups[tc % 2][0:w, :], nzsT["u"][:, 0:w],
                             sig_sb[:, bass.ts(tc, 512)],
                             start=False, stop=True).then_inc(sem["mmu"], 1)

        @block.scalar
        def _(sc):
            for si, (ch, c) in enumerate(slices):
                w = WID[ch]
                for mi, m in enumerate(("g", "u")):
                    k = 2 * si + mi
                    if si >= 1:
                        sc.wait_ge(sem["mmu"], 4 * si)
                    for half in range(2):
                        for gi in range(2):
                            gidx = 4 * k + 2 * half + gi
                            sc.wait_ge(sem["wtr"], gidx + 1)
                            sc.copy(W_sb[m][:, 16 * half + 8 * gi:16 * half + 8 * gi + 8, :],
                                    psW[gidx % 2][:]).then_inc(sem["wev"], 1)
                    sc.wait_ge(sem["ztr"], k + 1)
                    sc.copy(nzsT[m][:], psZ[:]).then_inc(sem["zev"], 1)
                for tc in range(4):
                    u = 4 * si + tc
                    sc.wait_ge(sem["mmg"], u + 1)
                    if u >= 1:
                        sc.wait_ge(sem["hml"], u)
                    sc.activation(out=sg_sb[0:w, :], in_=gps[tc % 2][0:w, :],
                                  func=AF.Silu).then_inc(sem["sil"], 1)

        @block.gpsimd
        def _(g):
            hv = h0.rearrange("(nw c8) (th t2) -> nw c8 th t2", c8=8, t2=1024)
            for si, (ch, c) in enumerate(slices):
                w = WID[ch]
                if c == 0:
                    for m, qT, sname in (("g", gqT, "wldg"), ("u", uqT, "wldu")):
                        if ch >= 1:
                            g.wait_ge(sem["tts"], 32 * ch)
                        g.dma_start(out=words[m][:],
                                    in_=qT[bass.ts(ch, 128), :]).then_inc(sem[sname], 16)
                for half in range(2):
                    g.wait_ge(sem["hml"], 4 * si + 2 * (half + 1))
                    g.dma_start(out=hv[128 * ch:128 * ch + w, c, half, :],
                                in_=h_st[si % 2][0:w, :, :]).then_inc(sem[f"hout{si % 2}"], 16)
            g.wait_ge(sem["hout0"], 32 * ((len(slices) + 1) // 2))
            g.wait_ge(sem["hout1"], 32 * (len(slices) // 2))

    return nc



def build_l0_out():
    nc = bass.Bass("TRN2", target_bir_lowering=False)
    h0 = nc.dram_tensor("h0", [11008, T], BF16, kind="ExternalInput").ap()
    ctx0 = nc.dram_tensor("ctx0", [D, T], BF16, kind="ExternalInput").ap()
    dqT = nc.dram_tensor("dqT", [512, 11008], I32, kind="ExternalInput").ap()
    Sd = nc.dram_tensor("Sd", [4, 128, 8, 86], F32, kind="ExternalInput").ap()
    zdT = nc.dram_tensor("zdT", [512, 86], I32, kind="ExternalInput").ap()
    woTp = nc.dram_tensor("woTp", [D, 4, 8, 128], F32, kind="ExternalInput").ap()
    xT = nc.dram_tensor("xT", [D, T], F32, kind="ExternalInput").ap()
    yA = nc.dram_tensor("yA", [D, T], F32, kind="ExternalOutput").ap()
    wdq = nc.dram_tensor("wdq", [4, 8, 86, 128, 128], BF16, kind="Internal").ap()
    sigd = nc.dram_tensor("sigd", [86, T], F32, kind="Internal").ap()

    NG = 86
    HALF = 43

    with contextlib.ExitStack() as top:
        identB = top.enter_context(nc.sbuf_tensor("identB", [128, 128], BF16))
        ones1 = top.enter_context(nc.sbuf_tensor("ones1", [128, 1], BF16))
        nzsT_d = top.enter_context(nc.sbuf_tensor("nzsT_d", [86, 32, 128], BF16))
        sig_sb = top.enter_context(nc.sbuf_tensor("sig_sb", [86, T], BF16))
        block = top.enter_context(nc.Block())
        sem = {n: top.enter_context(nc.semaphore(n)) for n in
               ["idn", "wldd", "zl", "tts", "wtr", "wev", "ztr", "zev", "wdw0", "wdw1",
                "sgh0", "sgh1", "sgm", "sge", "sgw", "sgl",
                "hld", "cld", "wdl0", "wdl1", "wol0", "wol1", "xtl0", "xtl1",
                "mm", "yev", "yout0", "yout1", "vss"]}
        vss = [0]

        def vbar(v, inst):
            vss[0] += 1
            inst.then_inc(sem["vss"], 1)
            v.wait_ge(sem["vss"], vss[0])

        @block.gpsimd
        def _(g):
            g.memset(identB[:], 1.0).then_inc(sem["idn"], 1)
            g.memset(ones1[:], 1.0).then_inc(sem["idn"], 1)
            g.wait_ge(sem["idn"], 2)
            g.affine_select(out=identB[:], in_=identB[:], compare_op=ALU.is_equal,
                            fill=0.0, base=0, pattern=[[-1, 128]],
                            channel_multiplier=1).then_inc(sem["idn"], 1)

        with contextlib.ExitStack() as stA:
            words_d = stA.enter_context(nc.sbuf_tensor("words_d", [128, 11008], I32))
            nib_d = stA.enter_context(nc.sbuf_tensor("nib_d", [128, HALF * 128], I32))
            wsc_d = stA.enter_context(nc.sbuf_tensor("wsc_d", [128, HALF * 128], BF16))
            S_sb = stA.enter_context(nc.sbuf_tensor("Sd_sb", [128, 8, 86], F32))
            z_sb = stA.enter_context(nc.sbuf_tensor("zd_sb", [128, 86], I32))
            znib = stA.enter_context(nc.sbuf_tensor("znibd", [128, 86], I32))
            nzs = stA.enter_context(nc.sbuf_tensor("nzsd", [128, 86], F32))
            nzs_bf = stA.enter_context(nc.sbuf_tensor("nzsd_bf", [128, 86], BF16))
            wstage = [stA.enter_context(nc.sbuf_tensor(f"wstage{i}", [128, 8, 128], BF16)) for i in range(2)]
            psW = [stA.enter_context(nc.psum_tensor(f"psWd{i}", [128, 8, 128], BF16)) for i in range(2)]
            psZ = stA.enter_context(nc.psum_tensor("psZd", [86, 128], BF16))

            GRP = [(gi * 8, min(8, HALF - gi * 8)) for gi in range((HALF + 7) // 8)]
            NGRP = len(GRP)

            @block.gpsimd
            def _(g):
                for oc in range(4):
                    if oc >= 1:
                        g.wait_ge(sem["tts"], 16 * oc)
                    g.dma_start(out=words_d[:], in_=dqT[bass.ts(oc, 128), :]).then_inc(sem["wldd"], 16)


            @block.vector
            def _(v):
                for oc in range(4):
                    for c in range(8):
                        k = 8 * oc + c
                        sh = 4 * AWQ_ORDER[c]
                        v.wait_ge(sem["wldd"], 16 * (oc + 1))
                        v.wait_ge(sem["zl"], 32 * (oc + 1))
                        for half in range(2):
                            if 2 * k + half >= 1:
                                v.wait_ge(sem["tts"], 2 * k + half)
                            vbar(v, v.tensor_scalar(out=nib_d[:], in0=words_d[:, bass.ts(half, HALF * 128)],
                                                    scalar1=sh, scalar2=15,
                                                    op0=ALU.logical_shift_right,
                                                    op1=ALU.bitwise_and))
                            v.wait_ge(sem["wtr"], NGRP * (2 * k + half))
                            v.tensor_tensor(wsc_d[:].rearrange("p (a b) -> p a b", b=128),
                                            nib_d[:].rearrange("p (a b) -> p a b", b=128),
                                            S_sb[:, c, HALF * half:HALF * (half + 1)][:, :, None].broadcast_to([128, HALF, 128]),
                                            ALU.mult).then_inc(sem["tts"], 1)
                        vbar(v, v.tensor_scalar(out=znib[:], in0=z_sb[:], scalar1=sh,
                                                scalar2=15, op0=ALU.logical_shift_right,
                                                op1=ALU.bitwise_and))
                        vbar(v, v.scalar_tensor_tensor(out=nzs[:], in0=znib[:], scalar=-1.0,
                                                       in1=S_sb[:, c, :], op0=ALU.mult,
                                                       op1=ALU.mult))
                        if k >= 1:
                            v.wait_ge(sem["ztr"], k)
                        vz = v.tensor_copy(nzs_bf[:], nzs[:])
                        vz.then_inc(sem["vss"], 1); vss[0] += 1
                        v.wait_ge(sem["vss"], vss[0])

            @block.tensor
            def _(t):
                t.wait_ge(sem["idn"], 3)
                for oc in range(4):
                    for c in range(8):
                        k = 8 * oc + c
                        for half in range(2):
                            t.wait_ge(sem["tts"], 2 * k + half + 1)
                            for gi, (g0, gn) in enumerate(GRP):
                                gidx = NGRP * (2 * k + half) + gi
                                if gidx >= 2:
                                    t.wait_ge(sem["wev"], gidx - 1)
                                for j in range(gn):
                                    ti = t.transpose(psW[gidx % 2][:, j, :],
                                                     wsc_d[:, bass.ts(g0 + j, 128)], identB[:])
                                ti.then_inc(sem["wtr"], 1)
                        t.wait_ge(sem["vss"], 5 * (k + 1))
                        if k >= 1:
                            t.wait_ge(sem["zev"], k)
                        t.transpose(psZ[:], nzs_bf[:], identB[:]).then_inc(sem["ztr"], 1)

            @block.scalar
            def _(sc):
                for oc in range(4):
                    for c in range(8):
                        k = 8 * oc + c
                        for half in range(2):
                            for gi, (g0, gn) in enumerate(GRP):
                                gidx = NGRP * (2 * k + half) + gi
                                sc.wait_ge(sem["wtr"], gidx + 1)
                                if gidx >= 2:
                                    sc.wait_ge(sem[f"wdw{gidx % 2}"], 16 * (gidx // 2))
                                sc.copy(wstage[gidx % 2][:, 0:gn, :],
                                        psW[gidx % 2][:, 0:gn, :]).then_inc(sem["wev"], 1)
                        sc.wait_ge(sem["ztr"], k + 1)
                        sc.copy(nzsT_d[:, k, :], psZ[:]).then_inc(sem["zev"], 1)

            @block.sync
            def _(s):
                for oc in range(4):
                    if oc >= 1:
                        s.wait_ge(sem["zev"], 8 * oc)
                    s.dma_start(out=z_sb[:], in_=zdT[bass.ts(oc, 128), :]).then_inc(sem["zl"], 16)
                    s.dma_start(out=S_sb[:], in_=Sd[oc]).then_inc(sem["zl"], 16)
                    for c in range(8):
                        k = 8 * oc + c
                        for half in range(2):
                            for gi, (g0, gn) in enumerate(GRP):
                                gidx = NGRP * (2 * k + half) + gi
                                s.wait_ge(sem["wev"], gidx + 1)
                                gg = HALF * half + g0
                                s.dma_start(out=wdq[oc, c, gg:gg + gn].rearrange("g p o -> p g o"),
                                            in_=wstage[gidx % 2][:, 0:gn, :]).then_inc(sem[f"wdw{gidx % 2}"], 16)

        with contextlib.ExitStack() as stB:
            hh = [stB.enter_context(nc.sbuf_tensor(f"hh{i}", [128, T], BF16)) for i in range(2)]
            sig_row = stB.enter_context(nc.sbuf_tensor("sig_rowd", [1, 512], F32))
            psSig = stB.enter_context(nc.psum_tensor("psSigd", [1, 512], F32))

            @block.gpsimd
            def _(g):
                g.wait_ge(sem["wdw0"], 16 * (4 * 8 * 2 * 6 // 2))
                g.wait_ge(sem["wdw1"], 16 * (4 * 8 * 2 * 6 // 2))
                for gg in range(NG):
                    if gg >= 2:
                        g.wait_ge(sem["sgm"], 4 * (gg - 1))
                    g.dma_start(out=hh[gg % 2][:], in_=h0[bass.ts(gg, 128), :]).then_inc(
                        sem[f"sgh{gg % 2}"], 16)

            @block.tensor
            def _(t):
                for gg in range(NG):
                    t.wait_ge(sem[f"sgh{gg % 2}"], 16 * (gg // 2 + 1))
                    for tc in range(4):
                        if 4 * gg + tc >= 1:
                            t.wait_ge(sem["sge"], 4 * gg + tc)
                        t.matmul(psSig[:], ones1[:], hh[gg % 2][:, bass.ts(tc, 512)],
                                 start=True, stop=True).then_inc(sem["sgm"], 1)

            @block.scalar
            def _(sc):
                for gg in range(NG):
                    for tc in range(4):
                        sc.wait_ge(sem["sgm"], 4 * gg + tc + 1)
                        if 4 * gg + tc >= 1:
                            sc.wait_ge(sem["sgw"], 16 * (4 * gg + tc))
                        sc.copy(sig_row[:], psSig[:]).then_inc(sem["sge"], 1)

            @block.sync
            def _(s):
                for gg in range(NG):
                    for tc in range(4):
                        s.wait_ge(sem["sge"], 4 * gg + tc + 1)
                        s.dma_start(out=sigd[gg:gg + 1, bass.ts(tc, 512)],
                                    in_=sig_row[:]).then_inc(sem["sgw"], 16)

            @block.gpsimd
            def _(g):
                g.wait_ge(sem["sgw"], 16 * 4 * NG)
                g.dma_start(out=sig_sb[:], in_=sigd).then_inc(sem["sgl"], 16)

        with contextlib.ExitStack() as stC:
            h_all = stC.enter_context(nc.sbuf_tensor("h_all", [128, NG, 512], BF16))
            ctx_tc = stC.enter_context(nc.sbuf_tensor("ctx_tc", [128, 32, 512], BF16))
            wd_sb = [stC.enter_context(nc.sbuf_tensor(f"wd_sb{i}", [128, NG, 128], BF16)) for i in range(2)]
            wo_sb = [stC.enter_context(nc.sbuf_tensor(f"wo_sb{i}", [128, 32, 128], BF16)) for i in range(2)]
            xt_sb = [stC.enter_context(nc.sbuf_tensor(f"xt_sb{i}", [128, 512], F32)) for i in range(2)]
            y_sb = [stC.enter_context(nc.sbuf_tensor(f"y_sb{i}", [128, 512], F32)) for i in range(2)]
            psY = [stC.enter_context(nc.psum_tensor(f"psY{i}", [128, 512], F32)) for i in range(2)]

            xv = xT.rearrange("(ow c8) t -> ow c8 t", c8=8)
            yv = yA.rearrange("(ow c8) t -> ow c8 t", c8=8)

            @block.gpsimd
            def _(g):
                g.wait_ge(sem["sgl"], 16)
                for tc in range(4):
                    if tc >= 1:
                        g.wait_ge(sem["mm"], 32 * tc)
                    g.dma_start(out=h_all[:], in_=h0.rearrange("(g p) t -> p g t", p=128)[:, :, bass.ts(tc, 512)]).then_inc(sem["hld"], 16)
                    g.dma_start(out=ctx_tc[:], in_=ctx0.rearrange("(g p) t -> p g t", p=128)[:, :, bass.ts(tc, 512)]).then_inc(sem["cld"], 16)
                    for so in range(32):
                        oc, c = so // 8, so % 8
                        j = 32 * tc + so
                        if j >= 2:
                            g.wait_ge(sem["mm"], j - 1)
                            g.wait_ge(sem["yev"], j - 1)
                        g.dma_start(out=wd_sb[so % 2][:],
                                    in_=wdq[oc, c].rearrange("g p o -> p g o")).then_inc(sem[f"wdl{so % 2}"], 16)
                        g.dma_start(out=wo_sb[so % 2][:],
                                    in_=woTp.rearrange("(ch p) a b o -> p ch a b o", p=128)[:, :, oc, c, :]).then_inc(sem[f"wol{so % 2}"], 16)
                        g.dma_start(out=xt_sb[so % 2][:],
                                    in_=xv[bass.ts(oc, 128), c, bass.ts(tc, 512)]).then_inc(sem[f"xtl{so % 2}"], 16)

            @block.tensor
            def _(t):
                for tc in range(4):
                    t.wait_ge(sem["hld"], 16 * (tc + 1))
                    t.wait_ge(sem["cld"], 16 * (tc + 1))
                    for so in range(32):
                        j = 32 * tc + so
                        b = so % 2
                        t.wait_ge(sem[f"wdl{b}"], 16 * (j // 2 + 1))
                        t.wait_ge(sem[f"wol{b}"], 16 * (j // 2 + 1))
                        if j >= 2:
                            t.wait_ge(sem["yev"], j - 1)
                        for gg in range(NG):
                            t.matmul(psY[j % 2][:], wd_sb[b][:, gg, :],
                                     h_all[:, gg, :], start=(gg == 0), stop=False)
                        t.matmul(psY[j % 2][:], nzsT_d[:, so, :],
                                 sig_sb[:, bass.ts(tc, 512)], start=False, stop=False)
                        for ch in range(32):
                            ti = t.matmul(psY[j % 2][:], wo_sb[b][:, ch, :],
                                          ctx_tc[:, ch, :], start=False,
                                          stop=(ch == 31))
                        ti.then_inc(sem["mm"], 1)

            @block.vector
            def _(v):
                for tc in range(4):
                    for so in range(32):
                        j = 32 * tc + so
                        b = so % 2
                        v.wait_ge(sem["mm"], j + 1)
                        v.wait_ge(sem[f"xtl{b}"], 16 * (j // 2 + 1))
                        if j >= 2:
                            v.wait_ge(sem[f"yout{j % 2}"], 16 * (j // 2))
                        v.tensor_add(y_sb[j % 2][:], psY[j % 2][:],
                                     xt_sb[b][:]).then_inc(sem["yev"], 1)

            @block.sync
            def _(s):
                for tc in range(4):
                    for so in range(32):
                        oc, c = so // 8, so % 8
                        j = 32 * tc + so
                        s.wait_ge(sem["yev"], j + 1)
                        s.dma_start(out=yv[bass.ts(oc, 128), c, bass.ts(tc, 512)],
                                    in_=y_sb[j % 2][:]).then_inc(sem[f"yout{j % 2}"], 16)
                s.wait_ge(sem["yout0"], 16 * 64)
                s.wait_ge(sem["yout1"], 16 * 64)

    return nc



def build_l1_pre():
    nc = bass.Bass("TRN2", target_bir_lowering=False)
    yA = nc.dram_tensor("yA", [D, T], F32, kind="ExternalInput").ap()
    wqkvT = nc.dram_tensor("wqkvT", [D, 6144], F32, kind="ExternalInput").ap()
    bqkv = nc.dram_tensor("bqkv", [128, 48], F32, kind="ExternalInput").ap()
    yn = nc.dram_tensor("yn", [D, T], BF16, kind="ExternalOutput").ap()
    q1 = nc.dram_tensor("q1", [2048, T], BF16, kind="ExternalOutput").ap()
    k1 = nc.dram_tensor("k1", [2048, T], BF16, kind="ExternalOutput").ap()
    v1 = nc.dram_tensor("v1", [2048, T], BF16, kind="ExternalOutput").ap()

    with contextlib.ExitStack() as top:
        yn_sb = top.enter_context(nc.sbuf_tensor("yn_sb", [128, 32, T], BF16))
        identB = top.enter_context(nc.sbuf_tensor("identB", [128, 128], BF16))
        identF = top.enter_context(nc.sbuf_tensor("identF", [128, 128], F32))
        cpool = top.enter_context(nc.sbuf_tensor("cpool", [128, 2], F32))
        block = top.enter_context(nc.Block())
        sem = {n: top.enter_context(nc.semaphore(n)) for n in
               ["ld0", "ld1", "ty", "ey", "sq", "var", "srt", "xnr", "tb", "eb",
                "yno", "idn", "bvl", "wld0", "wld1", "mmq", "evq",
                "qout0", "qout1", "vss", "sss"]}
        vss = [0]
        sss = [0]

        def vbar(v, inst):
            vss[0] += 1
            inst.then_inc(sem["vss"], 1)
            v.wait_ge(sem["vss"], vss[0])

        def sbar(sc, inst):
            sss[0] += 1
            inst.then_inc(sem["sss"], 1)
            sc.wait_ge(sem["sss"], sss[0])

        @block.gpsimd
        def _(g):
            g.memset(identB[:], 1.0).then_inc(sem["idn"], 1)
            g.memset(identF[:], 1.0).then_inc(sem["idn"], 1)
            g.wait_ge(sem["idn"], 2)
            g.affine_select(out=identB[:], in_=identB[:], compare_op=ALU.is_equal,
                            fill=0.0, base=0, pattern=[[-1, 128]], channel_multiplier=1)
            g.affine_select(out=identF[:], in_=identF[:], compare_op=ALU.is_equal,
                            fill=0.0, base=0, pattern=[[-1, 128]],
                            channel_multiplier=1)
            g.memset(cpool[:, 1:2], -0.5).then_inc(sem["idn"], 1)

        NT = 16
        with contextlib.ExitStack() as st0:
            ybuf = [st0.enter_context(nc.sbuf_tensor(f"ybuf{i}", [128, 32, 128], F32)) for i in range(2)]
            y_tok = st0.enter_context(nc.sbuf_tensor("y_tok", [128, D], F32))
            yn_tok = st0.enter_context(nc.sbuf_tensor("yn_tok", [128, D], BF16))
            xsq = st0.enter_context(nc.sbuf_tensor("xsq", [128, D], BF16))
            stat = [st0.enter_context(nc.sbuf_tensor(f"stat{i}", [128, 8], F32)) for i in range(2)]
            psF = [st0.enter_context(nc.psum_tensor(f"psF{i}", [128, 4, 128], F32)) for i in range(2)]
            psB = [st0.enter_context(nc.psum_tensor(f"psB{i}", [128, 8, 128], BF16)) for i in range(2)]

            yv = yA.rearrange("(ch p) t -> p ch t", p=128)

            @block.sync
            def _(s):
                for i in range(NT):
                    b = i % 2
                    if i >= 2:
                        s.wait_ge(sem["ty"], (i - 1) * 8)
                    s.dma_start(out=ybuf[b][:], in_=yv[:, :, bass.ts(i, 128)]).then_inc(sem[f"ld{b}"], 16)

            @block.tensor
            def _(t):
                t.wait_ge(sem["idn"], 3)
                for i in range(NT):
                    b = i % 2
                    t.wait_ge(sem[f"ld{b}"], 16 * (i // 2 + 1))
                    for gi in range(8):
                        if i * 8 + gi >= 2:
                            t.wait_ge(sem["ey"], i * 8 + gi - 1)
                        for j in range(4):
                            ch = gi * 4 + j
                            ti = t.transpose(psF[gi % 2][:, j, :],
                                             ybuf[b][:, ch, :], identF[:])
                        ti.then_inc(sem["ty"], 1)
                    t.wait_ge(sem["xnr"], i + 1)
                    for gi in range(4):
                        if i * 4 + gi >= 2:
                            t.wait_ge(sem["eb"], i * 4 + gi - 1)
                        for j in range(8):
                            ch = gi * 8 + j
                            ti = t.transpose(psB[gi % 2][:, j, :],
                                             yn_tok[:, bass.ts(ch, 128)], identB[:])
                        ti.then_inc(sem["tb"], 1)

            @block.vector
            def _(v):
                for i in range(NT):
                    b = i % 2
                    for gi in range(8):
                        v.wait_ge(sem["ty"], i * 8 + gi + 1)
                        if gi == 0 and i >= 1:
                            v.wait_ge(sem["xnr"], i)
                        v.tensor_copy(y_tok[:, 512 * gi:512 * (gi + 1)],
                                      psF[gi % 2][:]).then_inc(sem["ey"], 1)
                    v.wait_ge(sem["ey"], 8 * (i + 1))
                    if i >= 2:
                        v.wait_ge(sem["srt"], i - 1)
                    vbar(v, v.reduce_sum(stat[b][:, 0:1], y_tok[:], axis=AX.X))
                    v.wait_ge(sem["sq"], i + 1)
                    vbar(v, v.reduce_sum(stat[b][:, 1:2], xsq[:], axis=AX.X))
                    vbar(v, v.tensor_scalar_mul(stat[b][:, 2:3], stat[b][:, 0:1], 1.0 / D))
                    vbar(v, v.tensor_scalar(out=stat[b][:, 7:8], in0=stat[b][:, 2:3],
                                            scalar1=stat[b][:, 2:3], scalar2=-1e-5,
                                            op0=ALU.mult, op1=ALU.add))
                    v.scalar_tensor_tensor(out=stat[b][:, 3:4], in0=stat[b][:, 1:2],
                                           scalar=1.0 / D, in1=stat[b][:, 7:8],
                                           op0=ALU.mult, op1=ALU.subtract).then_inc(sem["var"], 1)
                    v.wait_ge(sem["srt"], i + 1)
                    vbar(v, v.scalar_tensor_tensor(out=stat[b][:, 6:7], in0=stat[b][:, 2:3],
                                                   scalar=-1.0, in1=stat[b][:, 5:6],
                                                   op0=ALU.mult, op1=ALU.mult))
                    v.tensor_scalar(out=yn_tok[:], in0=y_tok[:],
                                    scalar1=stat[b][:, 5:6], scalar2=stat[b][:, 6:7],
                                    op0=ALU.mult, op1=ALU.add).then_inc(sem["xnr"], 1)

            @block.scalar
            def _(sc):
                for i in range(NT):
                    b = i % 2
                    sc.wait_ge(sem["ey"], i * 8 + 8)
                    if i >= 1:
                        sc.wait_ge(sem["var"], i)
                    sc.activation(out=xsq[:], in_=y_tok[:], func=AF.Square).then_inc(sem["sq"], 1)
                    sc.wait_ge(sem["var"], i + 1)
                    sc.wait_ge(sem["idn"], 3)
                    sbar(sc, sc.activation(out=stat[b][:, 4:5], in_=stat[b][:, 3:4], func=AF.Ln))
                    sc.activation(out=stat[b][:, 5:6], in_=stat[b][:, 4:5], func=AF.Exp,
                                  scale=cpool[:, 1:2]).then_inc(sem["srt"], 1)
                    for gi in range(4):
                        sc.wait_ge(sem["tb"], i * 4 + gi + 1)
                        sc.copy(yn_sb[:, bass.ts(gi, 8), bass.ts(i, 128)],
                                psB[gi % 2][:]).then_inc(sem["eb"], 1)

            @block.sync
            def _(s):
                s.wait_ge(sem["eb"], NT * 4)
                s.dma_start(out=yn.rearrange("(ch p) t -> p ch t", p=128),
                            in_=yn_sb[:]).then_inc(sem["yno"], 16)

        with contextlib.ExitStack() as st1:
            wbuf = [st1.enter_context(nc.sbuf_tensor(f"wbuf{i}", [128, 32, 128], BF16)) for i in range(2)]
            qstage = [st1.enter_context(nc.sbuf_tensor(f"qstage{i}", [128, T], BF16)) for i in range(2)]
            bq_sb = st1.enter_context(nc.sbuf_tensor("bq_sb", [128, 48], F32))
            psQ = [st1.enter_context(nc.psum_tensor(f"psQ{i}", [128, 4, 512], F32)) for i in range(2)]

            jobs = ([(o, q1) for o in range(16)] + [(o, k1) for o in range(16)]
                    + [(o, v1) for o in range(16)])

            @block.sync
            def _(s):
                s.wait_ge(sem["yno"], 16)
                s.dma_start(out=bq_sb[:], in_=bqkv).then_inc(sem["bvl"], 16)

            @block.gpsimd
            def _(g):
                g.wait_ge(sem["yno"], 16)
                for j, (o, dest) in enumerate(jobs):
                    if j >= 2:
                        g.wait_ge(sem["mmq"], j - 1)
                    g.dma_start(out=wbuf[j % 2][:],
                                in_=wqkvT.rearrange("(ch p) o -> p ch o", p=128)[:, :, bass.ts(j, 128)]
                                ).then_inc(sem[f"wld{j % 2}"], 16)

            @block.tensor
            def _(t):
                t.wait_ge(sem["bvl"], 16)
                for j, (o, dest) in enumerate(jobs):
                    t.wait_ge(sem[f"wld{j % 2}"], 16 * (j // 2 + 1))
                    if j >= 2:
                        t.wait_ge(sem["evq"], (j - 1) * 4)
                    for tc in range(4):
                        for ch in range(32):
                            mi = t.matmul(psQ[j % 2][:, tc, :], wbuf[j % 2][:, ch, :],
                                          yn_sb[:, ch, bass.ts(tc, 512)],
                                          start=(ch == 0), stop=(ch == 31))
                    mi.then_inc(sem["mmq"], 1)

            @block.vector
            def _(v):
                for j, (o, dest) in enumerate(jobs):
                    v.wait_ge(sem["mmq"], j + 1)
                    if j >= 2:
                        v.wait_ge(sem[f"qout{j % 2}"], 16 * (j // 2))
                    for tc in range(4):
                        v.tensor_scalar_add(qstage[j % 2][:, bass.ts(tc, 512)],
                                            psQ[j % 2][:, tc, :],
                                            bq_sb[:, j:j + 1]).then_inc(sem["evq"], 1)

            @block.sync
            def _(s):
                for j, (o, dest) in enumerate(jobs):
                    s.wait_ge(sem["evq"], (j + 1) * 4)
                    s.dma_start(out=dest[bass.ts(o, 128), :],
                                in_=qstage[j % 2][:]).then_inc(sem[f"qout{j % 2}"], 16)
                s.wait_ge(sem["qout0"], 16 * 24)
                s.wait_ge(sem["qout1"], 16 * 24)

    return nc



def build_l1_attn(NH=8):
    nc = bass.Bass("TRN2", target_bir_lowering=False)
    q1 = nc.dram_tensor("q1", [2048, T], BF16, kind="ExternalInput").ap()
    k1 = nc.dram_tensor("k1", [2048, T], BF16, kind="ExternalInput").ap()
    v1 = nc.dram_tensor("v1", [2048, T], BF16, kind="ExternalInput").ap()
    ctx1 = nc.dram_tensor("ctx1", [2048, T], BF16, kind="ExternalOutput").ap()
    pr = nc.dram_tensor("pr", [2, T, T], BF16, kind="Internal").ap()

    with contextlib.ExitStack() as top:
        K_sb = top.enter_context(nc.sbuf_tensor("K_sb", [128, 16, T], BF16))
        Vt_sb = top.enter_context(nc.sbuf_tensor("Vt_sb", [128, 8, 16, 256], BF16))
        qh_sb = [top.enter_context(nc.sbuf_tensor(f"qh{i}", [128, 2, T], BF16)) for i in range(2)]
        probs_sb = [top.enter_context(nc.sbuf_tensor(f"pb{i}", [128, T], BF16)) for i in range(2)]
        pT_sb = [top.enter_context(nc.sbuf_tensor(f"pT{i}", [128, T], BF16)) for i in range(2)]
        ctx_st = top.enter_context(nc.sbuf_tensor("ctx_st", [128, 2, T], BF16))
        mask_sb = top.enter_context(nc.sbuf_tensor("mask_sb", [128, 128], F32))
        mx = [top.enter_context(nc.sbuf_tensor(f"mx{i}", [128, 2], F32)) for i in range(2)]
        sums = [top.enter_context(nc.sbuf_tensor(f"sums{i}", [128, 2], F32)) for i in range(2)]
        ps = [top.enter_context(nc.psum_tensor(f"ps{i}", [128, 4, 512], F32)) for i in range(2)]
        block = top.enter_context(nc.Block())
        sem = {n: top.enter_context(nc.semaphore(n)) for n in
               ["k", "vt", "qh0", "qh1", "sc", "msk", "mxs", "ex", "nm",
                "pw0", "pw1", "pt0", "pt1", "cx", "cev", "cout", "idn", "vss"]}
        vss = [0]

        def vbar(v, inst):
            vss[0] += 1
            inst.then_inc(sem["vss"], 1)
            v.wait_ge(sem["vss"], vss[0])

        @block.gpsimd
        def _(g):
            g.memset(mask_sb[:], 0.0).then_inc(sem["idn"], 1)
            g.wait_ge(sem["idn"], 1)
            g.affine_select(out=mask_sb[:], in_=mask_sb[:], compare_op=ALU.is_ge,
                            fill=-30000.0, base=0, pattern=[[-1, 128]],
                            channel_multiplier=1).then_inc(sem["idn"], 1)

        @block.sync
        def _(s):
            s.dma_start(out=K_sb[:], in_=k1.rearrange("(h p) t -> p h t", p=128)).then_inc(sem["k"], 16)
            for h in range(8):
                for half in range(2):
                    for j in range(16):
                        s.dma_start(out=Vt_sb[:, h, j, bass.ts(half, 128)],
                                    in_=v1[256 * h + 128 * half:256 * h + 128 * (half + 1),
                                           bass.ts(j, 128)],
                                    transpose=True).then_inc(sem["vt"], 16)

        @block.gpsimd
        def _(g):
            def ctx_write(hh):
                cv = ctx1.rearrange("(h f p) t -> h f p t", f=2, p=128)
                for half in range(2):
                    g.wait_ge(sem["cev"], 2 * hh + half + 1)
                    g.dma_start(out=cv[hh, half],
                                in_=ctx_st[:, half, :]).then_inc(sem["cout"], 16)

            for h in range(NH):
                if h >= 3:
                    ctx_write(h - 3)
                if h >= 2:
                    g.wait_ge(sem["sc"], 16 * (h - 1))
                g.dma_start(out=qh_sb[h % 2][:],
                            in_=q1.rearrange("(h f p) t -> p h f t", f=2, p=128)[:, h]
                            ).then_inc(sem[f"qh{h % 2}"], 16)
            for hh in range(max(0, NH - 3), NH):
                ctx_write(hh)
            g.wait_ge(sem["cout"], 32 * NH)

        @block.tensor
        def _(t):
            t.wait_ge(sem["k"], 16)
            t.wait_ge(sem["vt"], 16 * 256)
            for h in range(NH):
                t.wait_ge(sem[f"qh{h % 2}"], 16 * (h // 2 + 1))
                for qt in range(16):
                    m = 16 * h + qt
                    ext = 128 * (qt + 1)
                    if m >= 2:
                        t.wait_ge(sem["ex"], m - 1)
                    if qt == 1 and h >= 1:
                        t.wait_ge(sem["cev"], 2 * h)
                    nn = (ext + 511) // 512
                    for tc in range(nn):
                        wdt = min(512, ext - 512 * tc)
                        for hf in range(2):
                            mi = t.matmul(ps[m % 2][:, tc, 0:wdt],
                                          qh_sb[h % 2][:, hf, bass.ts(qt, 128)],
                                          K_sb[:, 2 * h + hf, 512 * tc:512 * tc + wdt],
                                          start=(hf == 0), stop=(hf == 1))
                    mi.then_inc(sem["sc"], 1)
                for half in range(2):
                    for kc in range(16):
                        n = 32 * h + 16 * half + kc
                        t.wait_ge(sem[f"pt{n % 2}"], 16 * (n // 2 + 1))
                        for tcq in range(4):
                            mi = t.matmul(ps[1][:, tcq, :],
                                          Vt_sb[:, h, kc, bass.ts(half, 128)],
                                          pT_sb[n % 2][:, bass.ts(tcq, 512)],
                                          start=(kc == 0), stop=(kc == 15))
                        mi.then_inc(sem["cx"], 1)

        @block.vector
        def _(v):
            for h in range(NH):
                for qt in range(16):
                    m = 16 * h + qt
                    b = m % 2
                    ext = 128 * (qt + 1)
                    v.wait_ge(sem["sc"], m + 1)
                    v.tensor_add(ps[b].rearrange("p a b -> p (a b)")[:, ext - 128:ext],
                                 ps[b].rearrange("p a b -> p (a b)")[:, ext - 128:ext],
                                 mask_sb[:]).then_inc(sem["msk"], 1)
                    v.wait_ge(sem["msk"], m + 1)
                    vbar(v, v.reduce_max(mx[b][:, 0:1],
                                         ps[b].rearrange("p a b -> p (a b)")[:, 0:ext],
                                         axis=AX.X))
                    v.tensor_scalar_mul(mx[b][:, 1:2], mx[b][:, 0:1], -1.0).then_inc(sem["mxs"], 1)
                    v.wait_ge(sem["ex"], m + 1)
                    vbar(v, v.reciprocal(sums[b][:, 1:2], sums[b][:, 0:1]))
                    if ext < T:
                        v.memset(probs_sb[b][:, ext:T], 0.0)
                    v.tensor_scalar_mul(probs_sb[b][:, 0:ext], probs_sb[b][:, 0:ext],
                                        sums[b][:, 1:2]).then_inc(sem["nm"], 1)

        @block.scalar
        def _(sc):
            for h in range(NH):
                for qt in range(16):
                    m = 16 * h + qt
                    b = m % 2
                    ext = 128 * (qt + 1)
                    sc.wait_ge(sem["mxs"], m + 1)
                    if m >= 2:
                        sc.wait_ge(sem[f"pw{m % 2}"], 16 * (m // 2))
                    sc.activation(out=probs_sb[b][:, 0:ext],
                                  in_=ps[b].rearrange("p a b -> p (a b)")[:, 0:ext],
                                  func=AF.Exp, bias=mx[b][:, 1:2],
                                  accum_out=sums[b][:, 0:1]).then_inc(sem["ex"], 1)
                for half in range(2):
                    for kc in range(16):
                        n = 32 * h + 16 * half + kc
                        sc.wait_ge(sem["pw0"], 16 * 8 * (h + 1))
                        sc.wait_ge(sem["pw1"], 16 * 8 * (h + 1))
                        if n >= 2:
                            sc.wait_ge(sem["cx"], n - 1)
                        sc.dma_start(out=pT_sb[n % 2][:],
                                     in_=pr[h % 2][:, bass.ts(kc, 128)],
                                     transpose=True).then_inc(sem[f"pt{n % 2}"], 16)
                    sc.wait_ge(sem["cx"], 32 * h + 16 * (half + 1))
                    if h >= 1:
                        sc.wait_ge(sem["cout"], 32 * h)
                    sc.copy(ctx_st[:, half, :],
                            ps[1].rearrange("p a b -> p (a b)")).then_inc(sem["cev"], 1)

        @block.sync
        def _(s):
            for h in range(NH):
                for qt in range(16):
                    m = 16 * h + qt
                    ext = 128 * (qt + 1)
                    s.wait_ge(sem["nm"], m + 1)
                    if qt == 0 and h >= 2:
                        s.wait_ge(sem["cx"], 32 * (h - 1))
                    s.dma_start(out=pr[h % 2, bass.ts(qt, 128), :],
                                in_=probs_sb[m % 2][:]).then_inc(sem[f"pw{m % 2}"], 16)

    return nc



def build_l1_ffn():
    nc = bass.Bass("TRN2", target_bir_lowering=False)
    yn = nc.dram_tensor("yn", [D, T], BF16, kind="ExternalInput").ap()
    U1 = nc.dram_tensor("U1", [D, 1024], F32, kind="ExternalInput").ap()
    V1 = nc.dram_tensor("V1", [1024, 22016], F32, kind="ExternalInput").ap()
    b1 = nc.dram_tensor("b1", [128, 172], F32, kind="ExternalInput").ap()
    U2 = nc.dram_tensor("U2", [11008, 1024], F32, kind="ExternalInput").ap()
    z1 = nc.dram_tensor("z1", [1024, T], BF16, kind="ExternalOutput").ap()
    act1 = nc.dram_tensor("act1", [11008, T], BF16, kind="Internal").ap()

    with contextlib.ExitStack() as top:
        P_sb = top.enter_context(nc.sbuf_tensor("P_sb", [128, 8, T], BF16))
        b1_sb = top.enter_context(nc.sbuf_tensor("b1_sb", [128, 172], F32))
        block = top.enter_context(nc.Block())
        sem = {n: top.enter_context(nc.semaphore(n)) for n in
               ["ynl", "b1l", "u1l0", "u1l1", "mmp", "pev", "pout",
                "v1g0", "v1g1", "v1u0", "v1u1", "mmg", "mmu", "sil", "aml",
                "aout0", "aout1", "atl", "u2l0", "u2l1", "mmz", "zev",
                "zout0", "zout1", "vss"]}
        vss = [0]

        with contextlib.ExitStack() as stA:
            yn_sb = stA.enter_context(nc.sbuf_tensor("yn_sb", [128, 32, T], BF16))
            u1buf = [stA.enter_context(nc.sbuf_tensor(f"u1b{i}", [128, 32, 128], BF16)) for i in range(2)]
            psP = [stA.enter_context(nc.psum_tensor(f"psP{i}", [128, 4, 512], F32)) for i in range(2)]

            @block.sync
            def _(s):
                s.dma_start(out=yn_sb[:], in_=yn.rearrange("(ch p) t -> p ch t", p=128)).then_inc(sem["ynl"], 16)
                s.dma_start(out=b1_sb[:], in_=b1).then_inc(sem["b1l"], 16)

            @block.gpsimd
            def _(g):
                for rs in range(8):
                    if rs >= 2:
                        g.wait_ge(sem["mmp"], rs - 1)
                    g.dma_start(out=u1buf[rs % 2][:],
                                in_=U1.rearrange("(ch p) r -> p ch r", p=128)[:, :, bass.ts(rs, 128)]
                                ).then_inc(sem[f"u1l{rs % 2}"], 16)

            @block.tensor
            def _(t):
                t.wait_ge(sem["ynl"], 16)
                for rs in range(8):
                    t.wait_ge(sem[f"u1l{rs % 2}"], 16 * (rs // 2 + 1))
                    if rs >= 2:
                        t.wait_ge(sem["pev"], (rs - 1) * 4)
                    for tc in range(4):
                        for ch in range(32):
                            mi = t.matmul(psP[rs % 2][:, tc, :], u1buf[rs % 2][:, ch, :],
                                          yn_sb[:, ch, bass.ts(tc, 512)],
                                          start=(ch == 0), stop=(ch == 31))
                    mi.then_inc(sem["mmp"], 1)

            @block.vector
            def _(v):
                for rs in range(8):
                    v.wait_ge(sem["mmp"], rs + 1)
                    for tc in range(4):
                        v.tensor_copy(P_sb[:, rs, bass.ts(tc, 512)],
                                      psP[rs % 2][:, tc, :]).then_inc(sem["pev"], 1)

        with contextlib.ExitStack() as stB:
            v1b = {m: [stB.enter_context(nc.sbuf_tensor(f"v1b_{m}{i}", [128, 8, 128], BF16))
                       for i in range(2)] for m in "gu"}
            sg_sb = stB.enter_context(nc.sbuf_tensor("sg_sb", [128, 512], BF16))
            a_st = [stB.enter_context(nc.sbuf_tensor(f"a_st{i}", [128, 2, 512], BF16)) for i in range(2)]
            gps = [stB.enter_context(nc.psum_tensor(f"gps{i}", [128, 512], F32)) for i in range(2)]
            ups = [stB.enter_context(nc.psum_tensor(f"ups{i}", [128, 512], F32)) for i in range(2)]

            NJ = 86
            v1v = V1.rearrange("(ch p) f -> p ch f", p=128)

            @block.gpsimd
            def _(g):
                g.wait_ge(sem["pev"], 32)
                for j in range(NJ):
                    for m, base, sname in (("g", 0, "v1g"), ("u", 11008, "v1u")):
                        if j >= 2:
                            g.wait_ge(sem["mmu"], 4 * (j - 1))
                        g.dma_start(out=v1b[m][j % 2][:],
                                    in_=v1v[:, :, base + 128 * j:base + 128 * (j + 1)]
                                    ).then_inc(sem[f"{sname}{j % 2}"], 16)

            @block.tensor
            def _(t):
                for j in range(NJ):
                    t.wait_ge(sem[f"v1g{j % 2}"], 16 * (j // 2 + 1))
                    t.wait_ge(sem[f"v1u{j % 2}"], 16 * (j // 2 + 1))
                    for tc in range(4):
                        u = 4 * j + tc
                        if u >= 2:
                            t.wait_ge(sem["sil"], u - 1)
                            t.wait_ge(sem["aml"], u - 1)
                        for ch in range(8):
                            mi = t.matmul(gps[tc % 2][:], v1b["g"][j % 2][:, ch, :],
                                          P_sb[:, ch, bass.ts(tc, 512)],
                                          start=(ch == 0), stop=(ch == 7))
                        mi.then_inc(sem["mmg"], 1)
                        for ch in range(8):
                            mi = t.matmul(ups[tc % 2][:], v1b["u"][j % 2][:, ch, :],
                                          P_sb[:, ch, bass.ts(tc, 512)],
                                          start=(ch == 0), stop=(ch == 7))
                        mi.then_inc(sem["mmu"], 1)

            @block.scalar
            def _(sc):
                sc.wait_ge(sem["b1l"], 16)
                for j in range(NJ):
                    for tc in range(4):
                        u = 4 * j + tc
                        sc.wait_ge(sem["mmg"], u + 1)
                        if u >= 1:
                            sc.wait_ge(sem["aml"], u)
                        sc.activation(out=sg_sb[:], in_=gps[tc % 2][:], func=AF.Silu,
                                      bias=b1_sb[:, j:j + 1]).then_inc(sem["sil"], 1)

            @block.vector
            def _(v):
                for j in range(NJ):
                    for tc in range(4):
                        u = 4 * j + tc
                        v.wait_ge(sem["sil"], u + 1)
                        v.wait_ge(sem["mmu"], u + 1)
                        if tc == 0 and j >= 2:
                            v.wait_ge(sem[f"aout{j % 2}"], 32 * (j // 2))
                        if tc == 2:
                            v.wait_ge(sem[f"aout{j % 2}"], 32 * (j // 2) + 16)
                        v.scalar_tensor_tensor(out=a_st[j % 2][:, tc % 2, :],
                                               in0=ups[tc % 2][:],
                                               scalar=b1_sb[:, 86 + j:87 + j],
                                               in1=sg_sb[:], op0=ALU.add,
                                               op1=ALU.mult).then_inc(sem["aml"], 1)

            @block.sync
            def _(s):
                av = act1.rearrange("n (th t2) -> n th t2", t2=1024)
                for j in range(NJ):
                    for half in range(2):
                        s.wait_ge(sem["aml"], 4 * j + 2 * (half + 1))
                        s.dma_start(out=av[bass.ts(j, 128), half, :],
                                    in_=a_st[j % 2][:, :, :]).then_inc(sem[f"aout{j % 2}"], 16)

        with contextlib.ExitStack() as stC:
            act_tc = stC.enter_context(nc.sbuf_tensor("act_tc", [128, 86, 512], BF16))
            u2buf = [stC.enter_context(nc.sbuf_tensor(f"u2b{i}", [128, 86, 128], BF16)) for i in range(2)]
            zst = [stC.enter_context(nc.sbuf_tensor(f"zst{i}", [128, 512], BF16)) for i in range(2)]
            psZ = [stC.enter_context(nc.psum_tensor(f"psZz{i}", [128, 512], F32)) for i in range(2)]

            @block.gpsimd
            def _(g):
                g.wait_ge(sem["aout0"], 32 * 43)
                g.wait_ge(sem["aout1"], 32 * 43)
                for tc in range(4):
                    if tc >= 1:
                        g.wait_ge(sem["mmz"], 8 * tc)
                    g.dma_start(out=act_tc[:],
                                in_=act1.rearrange("(g p) t -> p g t", p=128)[:, :, bass.ts(tc, 512)]
                                ).then_inc(sem["atl"], 16)
                    for rs in range(8):
                        j = 8 * tc + rs
                        if j >= 2:
                            g.wait_ge(sem["mmz"], j - 1)
                        g.dma_start(out=u2buf[rs % 2][:],
                                    in_=U2.rearrange("(g p) r -> p g r", p=128)[:, :, bass.ts(rs, 128)]
                                    ).then_inc(sem[f"u2l{rs % 2}"], 16)

            @block.tensor
            def _(t):
                for tc in range(4):
                    t.wait_ge(sem["atl"], 16 * (tc + 1))
                    for rs in range(8):
                        j = 8 * tc + rs
                        t.wait_ge(sem[f"u2l{rs % 2}"], 16 * (j // 2 + 1))
                        if j >= 2:
                            t.wait_ge(sem["zev"], j - 1)
                        for gg in range(86):
                            mi = t.matmul(psZ[j % 2][:], u2buf[rs % 2][:, gg, :],
                                          act_tc[:, gg, :], start=(gg == 0),
                                          stop=(gg == 85))
                        mi.then_inc(sem["mmz"], 1)

            @block.vector
            def _(v):
                for tc in range(4):
                    for rs in range(8):
                        j = 8 * tc + rs
                        v.wait_ge(sem["mmz"], j + 1)
                        if j >= 2:
                            v.wait_ge(sem[f"zout{j % 2}"], 16 * (j // 2))
                        v.tensor_copy(zst[j % 2][:], psZ[j % 2][:]).then_inc(sem["zev"], 1)

            @block.sync
            def _(s):
                for tc in range(4):
                    for rs in range(8):
                        j = 8 * tc + rs
                        s.wait_ge(sem["zev"], j + 1)
                        s.dma_start(out=z1[bass.ts(rs, 128), bass.ts(tc, 512)],
                                    in_=zst[j % 2][:]).then_inc(sem[f"zout{j % 2}"], 16)
                s.wait_ge(sem["zout0"], 16 * 16)
                s.wait_ge(sem["zout1"], 16 * 16)

    return nc



def build_l1_out():
    nc = bass.Bass("TRN2", target_bir_lowering=False)
    z1 = nc.dram_tensor("z1", [1024, T], BF16, kind="ExternalInput").ap()
    ctx1 = nc.dram_tensor("ctx1", [2048, T], BF16, kind="ExternalInput").ap()
    V2 = nc.dram_tensor("V2", [1024, D], F32, kind="ExternalInput").ap()
    woT = nc.dram_tensor("woT", [2048, D], F32, kind="ExternalInput").ap()
    badd2 = nc.dram_tensor("badd2", [128, 32, 2], F32, kind="ExternalInput").ap()
    yA = nc.dram_tensor("yA", [D, T], F32, kind="ExternalInput").ap()
    out = nc.dram_tensor("out", [T, D], F32, kind="ExternalOutput").ap()

    with contextlib.ExitStack() as top:
        z_sb = top.enter_context(nc.sbuf_tensor("z_sb", [128, 8, T], BF16))
        ctx_sb = top.enter_context(nc.sbuf_tensor("ctx_sb", [128, 16, T], BF16))
        identF = top.enter_context(nc.sbuf_tensor("identF", [128, 128], F32))
        badd_sb = top.enter_context(nc.sbuf_tensor("badd_sb", [128, 32, 2], F32))
        badd = top.enter_context(nc.sbuf_tensor("badd", [128, 32], F32))
        v2buf = [top.enter_context(nc.sbuf_tensor(f"v2b{i}", [128, 8, 128], BF16)) for i in range(2)]
        wobuf = [top.enter_context(nc.sbuf_tensor(f"wob{i}", [128, 16, 128], BF16)) for i in range(2)]
        ya_sb = [top.enter_context(nc.sbuf_tensor(f"ya{i}", [128, 512], F32)) for i in range(2)]
        y2_sb = [top.enter_context(nc.sbuf_tensor(f"y2{i}", [128, 512], F32)) for i in range(2)]
        ostage = [top.enter_context(nc.sbuf_tensor(f"ost{i}", [128, 4, 128], F32)) for i in range(2)]
        psY = [top.enter_context(nc.psum_tensor(f"psY{i}", [128, 512], F32)) for i in range(2)]
        psT = [top.enter_context(nc.psum_tensor(f"psT{i}", [128, 4, 128], F32)) for i in range(2)]
        block = top.enter_context(nc.Block())
        sem = {n: top.enter_context(nc.semaphore(n)) for n in
               ["zl", "cl", "bl", "idn", "v2l0", "v2l1", "wol0", "wol1",
                "yal0", "yal1", "mm", "yev", "ttr", "tev", "oout0", "oout1", "vss"]}
        vss = [0]

        def vbar(v, inst):
            vss[0] += 1
            inst.then_inc(sem["vss"], 1)
            v.wait_ge(sem["vss"], vss[0])

        @block.gpsimd
        def _(g):
            g.memset(identF[:], 1.0).then_inc(sem["idn"], 1)
            g.wait_ge(sem["idn"], 1)
            g.affine_select(out=identF[:], in_=identF[:], compare_op=ALU.is_equal,
                            fill=0.0, base=0, pattern=[[-1, 128]],
                            channel_multiplier=1).then_inc(sem["idn"], 1)

        @block.sync
        def _(s):
            s.dma_start(out=z_sb[:], in_=z1.rearrange("(ch p) t -> p ch t", p=128)).then_inc(sem["zl"], 16)
            s.dma_start(out=ctx_sb[:], in_=ctx1.rearrange("(ch p) t -> p ch t", p=128)).then_inc(sem["cl"], 16)
            s.dma_start(out=badd_sb[:], in_=badd2).then_inc(sem["bl"], 16)

        @block.vector
        def _(v):
            v.wait_ge(sem["bl"], 16)
            vbar(v, v.tensor_add(badd[:], badd_sb[:, :, 0], badd_sb[:, :, 1]))
            for tc in range(4):
                for os in range(32):
                    j = 32 * tc + os
                    b = j % 2
                    v.wait_ge(sem["mm"], j + 1)
                    v.wait_ge(sem[f"yal{b}"], 16 * (j // 2 + 1))
                    if j >= 2:
                        v.wait_ge(sem["ttr"], j - 1)
                    v.scalar_tensor_tensor(out=y2_sb[b][:], in0=psY[b][:],
                                           scalar=badd[:, os:os + 1], in1=ya_sb[b][:],
                                           op0=ALU.add, op1=ALU.add).then_inc(sem["yev"], 1)
                    if j >= 1:
                        jj = j - 1
                        v.wait_ge(sem["ttr"], jj + 1)
                        if jj >= 2:
                            v.wait_ge(sem[f"oout{jj % 2}"], 16 * (jj // 2))
                        v.tensor_copy(ostage[jj % 2][:], psT[jj % 2][:]).then_inc(sem["tev"], 1)
            jj = 127
            v.wait_ge(sem["ttr"], 128)
            v.wait_ge(sem[f"oout{jj % 2}"], 16 * (jj // 2))
            v.tensor_copy(ostage[jj % 2][:], psT[jj % 2][:]).then_inc(sem["tev"], 1)

        @block.gpsimd
        def _(g):
            for tc in range(4):
                for os in range(32):
                    j = 32 * tc + os
                    b = j % 2
                    if j >= 2:
                        g.wait_ge(sem["mm"], j - 1)
                        g.wait_ge(sem["yev"], j - 1)
                    g.dma_start(out=v2buf[b][:],
                                in_=V2.rearrange("(ch p) o -> p ch o", p=128)[:, :, bass.ts(os, 128)]
                                ).then_inc(sem[f"v2l{b}"], 16)
                    g.dma_start(out=wobuf[b][:],
                                in_=woT.rearrange("(ch p) o -> p ch o", p=128)[:, :, bass.ts(os, 128)]
                                ).then_inc(sem[f"wol{b}"], 16)
                    g.dma_start(out=ya_sb[b][:],
                                in_=yA[bass.ts(os, 128), bass.ts(tc, 512)]).then_inc(sem[f"yal{b}"], 16)

        @block.tensor
        def _(t):
            t.wait_ge(sem["zl"], 16)
            t.wait_ge(sem["cl"], 16)
            t.wait_ge(sem["idn"], 2)
            for tc in range(4):
                for os in range(32):
                    j = 32 * tc + os
                    b = j % 2
                    t.wait_ge(sem[f"v2l{b}"], 16 * (j // 2 + 1))
                    t.wait_ge(sem[f"wol{b}"], 16 * (j // 2 + 1))
                    if j >= 2:
                        t.wait_ge(sem["yev"], j - 1)
                    for ch in range(8):
                        t.matmul(psY[b][:], v2buf[b][:, ch, :],
                                 z_sb[:, ch, bass.ts(tc, 512)],
                                 start=(ch == 0), stop=False)
                    for ch in range(16):
                        mi = t.matmul(psY[b][:], wobuf[b][:, ch, :],
                                      ctx_sb[:, ch, bass.ts(tc, 512)],
                                      start=False, stop=(ch == 15))
                    mi.then_inc(sem["mm"], 1)
                    if j >= 1:
                        jj = j - 1
                        t.wait_ge(sem["yev"], jj + 1)
                        if jj >= 2:
                            t.wait_ge(sem["tev"], jj - 1)
                        for sub in range(4):
                            ti = t.transpose(psT[jj % 2][:, sub, :],
                                             y2_sb[jj % 2][:, bass.ts(sub, 128)],
                                             identF[:])
                        ti.then_inc(sem["ttr"], 1)
            jj = 127
            t.wait_ge(sem["yev"], 128)
            t.wait_ge(sem["tev"], 126)
            for sub in range(4):
                ti = t.transpose(psT[jj % 2][:, sub, :],
                                 y2_sb[jj % 2][:, bass.ts(sub, 128)], identF[:])
            ti.then_inc(sem["ttr"], 1)

        @block.sync
        def _(s):
            ov = out.rearrange("(a j p) o -> p a j o", j=4, p=128)
            for tc in range(4):
                for os in range(32):
                    j = 32 * tc + os
                    s.wait_ge(sem["tev"], j + 1)
                    s.dma_start(out=ov[:, tc, :, bass.ts(os, 128)],
                                in_=ostage[j % 2][:]).then_inc(sem[f"oout{j % 2}"], 16)
            s.wait_ge(sem["oout0"], 16 * 64)
            s.wait_ge(sem["oout1"], 16 * 64)

    return nc



def _t(a):
    return np.ascontiguousarray(np.asarray(a).T)


def _prep_awq_gu(qw, scales, qz):
    qw = np.asarray(qw, dtype=np.int32)
    qz = np.asarray(qz, dtype=np.int32)
    sc = np.asarray(scales, dtype=np.float32)
    qT = np.zeros((1408, qw.shape[0]), np.int32)
    qT[:1376] = qw.T
    zT = np.zeros((1408, 32), np.int32)
    zT[:1376] = qz.T
    scv = sc.reshape(32, 1376, 8)
    S = np.zeros((11, 128, 8, 32), np.float32)
    full = scv.transpose(1, 2, 0)
    S.reshape(1408, 8, 32)[:1376] = full
    return qT, np.ascontiguousarray(S), zT


def host_prep(inp):
    d = {}
    d["x"] = np.asarray(inp["L0_attn_hidden_states"], dtype=np.float32)
    d["wqT"] = _t(inp["L0_attn_Wq"])
    d["wkT"] = _t(inp["L0_attn_Wk"])
    d["wvT"] = _t(inp["L0_attn_Wv"])
    d["bv"] = np.ascontiguousarray(np.asarray(inp["L0_attn_bv"], dtype=np.float32).reshape(8, 128).T)
    d["gqT"], d["Sg"], d["zgT"] = _prep_awq_gu(
        inp["L0_ffn_gate_qweight"], inp["L0_ffn_gate_scales"], inp["L0_ffn_gate_qzeros"])
    d["uqT"], d["Su"], d["zuT"] = _prep_awq_gu(
        inp["L0_ffn_up_qweight"], inp["L0_ffn_up_scales"], inp["L0_ffn_up_qzeros"])
    d["dqT"] = np.ascontiguousarray(np.asarray(inp["L0_ffn_down_qweight"], np.int32).T)
    d["zdT"] = np.ascontiguousarray(np.asarray(inp["L0_ffn_down_qzeros"], np.int32).T)
    scd = np.asarray(inp["L0_ffn_down_scales"], np.float32).reshape(86, 512, 8)
    d["Sd"] = np.ascontiguousarray(scd.transpose(1, 2, 0).reshape(4, 128, 8, 86))
    woT = _t(inp["L0_attn_Wo"])
    d["woTp"] = np.ascontiguousarray(
        woT.reshape(4096, 4, 128, 8).transpose(0, 1, 3, 2))
    d["wqkvT"] = _t(inp["L1_attn_w_qkv"])
    d["bqkv"] = np.ascontiguousarray(np.asarray(inp["L1_attn_b_qkv"], np.float32).reshape(48, 128).T)
    d["U1"] = np.ascontiguousarray(np.asarray(inp["L1_ffn_U1"], np.float32))
    d["V1"] = np.ascontiguousarray(np.asarray(inp["L1_ffn_V1"], np.float32))
    d["b1"] = np.ascontiguousarray(np.asarray(inp["L1_ffn_b1"], np.float32).reshape(172, 128).T)
    d["U2"] = np.ascontiguousarray(np.asarray(inp["L1_ffn_U2"], np.float32))
    d["V2"] = np.ascontiguousarray(np.asarray(inp["L1_ffn_V2"], np.float32))
    d["woT1"] = _t(inp["L1_attn_w_o"])
    b2 = np.asarray(inp["L1_ffn_b2"], np.float32).reshape(32, 128).T
    bo = np.asarray(inp["L1_attn_b_o"], np.float32).reshape(32, 128).T
    d["badd2"] = np.ascontiguousarray(np.stack([b2, bo], axis=-1))
    return d


def kernel(**inp):
    d = host_prep(inp)
    x_full = d["x"]
    R = list(range(NCORES))
    bfv = ml_dtypes.bfloat16

    def bf16(rs, key):
        return [np.asarray(r[key], dtype=bfv) for r in rs]

    nc1 = build_l0_pre()
    r1 = run_bass_kernel_spmd(nc1, [
        {"x": np.ascontiguousarray(x_full[i]), "wqT": d["wqT"], "wkT": d["wkT"],
         "wvT": d["wvT"], "bv": d["bv"]} for i in R], core_ids=R).results

    nc2 = build_l0_attn()
    r2 = run_bass_kernel_spmd(nc2, [
        {"q0": np.asarray(r1[i]["q0"], bfv), "k0": np.asarray(r1[i]["k0"], bfv),
         "v0": np.asarray(r1[i]["v0"], bfv)} for i in R], core_ids=R).results

    nc3 = build_l0_ffn1()
    r3 = run_bass_kernel_spmd(nc3, [
        {"xn": np.asarray(r1[i]["xn"], bfv), "gqT": d["gqT"], "uqT": d["uqT"],
         "Sg": d["Sg"], "Su": d["Su"], "zgT": d["zgT"], "zuT": d["zuT"]}
        for i in R], core_ids=R).results

    nc4 = build_l0_out()
    r4 = run_bass_kernel_spmd(nc4, [
        {"h0": np.asarray(r3[i]["h0"], bfv), "ctx0": np.asarray(r2[i]["ctx0"], bfv),
         "dqT": d["dqT"], "Sd": d["Sd"], "zdT": d["zdT"], "woTp": d["woTp"],
         "xT": np.asarray(r1[i]["xT"], np.float32)} for i in R], core_ids=R).results

    nc5 = build_l1_pre()
    r5 = run_bass_kernel_spmd(nc5, [
        {"yA": np.asarray(r4[i]["yA"], np.float32), "wqkvT": d["wqkvT"],
         "bqkv": d["bqkv"]} for i in R], core_ids=R).results

    nc6 = build_l1_attn()
    r6 = run_bass_kernel_spmd(nc6, [
        {"q1": np.asarray(r5[i]["q1"], bfv), "k1": np.asarray(r5[i]["k1"], bfv),
         "v1": np.asarray(r5[i]["v1"], bfv)} for i in R], core_ids=R).results

    nc7 = build_l1_ffn()
    r7 = run_bass_kernel_spmd(nc7, [
        {"yn": np.asarray(r5[i]["yn"], bfv), "U1": d["U1"], "V1": d["V1"],
         "b1": d["b1"], "U2": d["U2"]} for i in R], core_ids=R).results

    nc8 = build_l1_out()
    r8 = run_bass_kernel_spmd(nc8, [
        {"z1": np.asarray(r7[i]["z1"], bfv), "ctx1": np.asarray(r6[i]["ctx1"], bfv),
         "V2": d["V2"], "woT": d["woT1"], "badd2": d["badd2"],
         "yA": np.asarray(r4[i]["yA"], np.float32)} for i in R], core_ids=R).results

    return np.stack([np.asarray(r8[i]["out"], np.float32) for i in R], axis=0)


# version 04 (final)
# speedup vs baseline: 1.0150x; 1.0150x over previous
import contextlib
import numpy as np
import ml_dtypes
import concourse.env as _env
_env.get_walrus_max_sem_num = lambda: 80
import concourse.bass as bass
bass.get_walrus_max_sem_num = _env.get_walrus_max_sem_num
import concourse.mybir as mybir
from concourse.bass_utils import run_bass_kernel_spmd

F32 = mybir.dt.float32
BF16 = mybir.dt.bfloat16
I32 = mybir.dt.int32
AF = mybir.ActivationFunctionType
ALU = mybir.AluOpType
AX = mybir.AxisListType

T = 2048
D = 4096
NCORES = 8
QSCALE = 128.0 ** -0.5
AWQ_ORDER = [0, 4, 1, 5, 2, 6, 3, 7]

bf = ml_dtypes.bfloat16




class FusedCtx:
    def __init__(self, nc, block, gstack):
        self.nc = nc
        self.block = block
        self.gstack = gstack
        self.fence = []

    def gate(self, eng):
        for s, v in self.fence:
            eng.wait_ge(s, v)


def _mk_standalone(name, decls, emit, extra_args=None):
    nc = bass.Bass("TRN2", target_bir_lowering=False)
    io = {}
    for nm, shape, dt, kind in decls:
        io[nm] = nc.dram_tensor(nm, shape, dt, kind=kind).ap()
    with contextlib.ExitStack() as g:
        block = g.enter_context(nc.Block())
        ctx = FusedCtx(nc, block, g)
        emit(ctx, io, **(extra_args or {}))
    return nc

def emit_l0_pre(ctx, io):
    nc = ctx.nc
    block = ctx.block
    PFX = "l0_pre:"
    x = io["x"]
    wqT = io["wqT"]
    wkT = io["wkT"]
    wvT = io["wvT"]
    bv = io["bv"]
    xT = io["xT"]
    xn = io["xn"]
    q0 = io["q0"]
    k0 = io["k0"]
    v0 = io["v0"]
    with contextlib.ExitStack() as top:
        xn_sb = top.enter_context(nc.sbuf_tensor(PFX + "xn_sb", [128, 32, T], BF16))
        identB = top.enter_context(nc.sbuf_tensor(PFX + "identB", [128, 128], BF16))
        identF = top.enter_context(nc.sbuf_tensor(PFX + "identF", [128, 128], F32))
        cpool = top.enter_context(nc.sbuf_tensor(PFX + "cpool", [128, 2], F32))
        sem = {n: nc.alloc_semaphore(name="l0_pre:" + n) for n in
               ["ld0", "ld1", "sq", "var", "srt", "xnr", "tb", "tf", "eb", "ef",
                "xto", "xno", "idn", "bvl", "wld0", "wld1", "mmq", "evq",
                "qout0", "qout1", "vss", "sss"]}
        vss = [0]
        sss = [0]

        def vbar(v, inst):
            vss[0] += 1
            inst.then_inc(sem["vss"], 1)
            v.wait_ge(sem["vss"], vss[0])

        def sbar(sc, inst):
            sss[0] += 1
            inst.then_inc(sem["sss"], 1)
            sc.wait_ge(sem["sss"], sss[0])

        @block.gpsimd
        def _(g):
            ctx.gate(g)
            g.memset(identB[:], 1.0).then_inc(sem["idn"], 1)
            g.memset(identF[:], 1.0).then_inc(sem["idn"], 1)
            g.wait_ge(sem["idn"], 2)
            g.affine_select(out=identB[:], in_=identB[:], compare_op=ALU.is_equal,
                            fill=0.0, base=0, pattern=[[-1, 128]], channel_multiplier=1)
            g.affine_select(out=identF[:], in_=identF[:], compare_op=ALU.is_equal,
                            fill=0.0, base=0, pattern=[[-1, 128]],
                            channel_multiplier=1)
            g.memset(cpool[:, 0:1], QSCALE)
            g.memset(cpool[:, 1:2], -0.5).then_inc(sem["idn"], 1)

        NT = 16
        with contextlib.ExitStack() as st0:
            x_tok = [st0.enter_context(nc.sbuf_tensor(PFX + f"x_tok{i}", [128, D], F32)) for i in range(2)]
            xn_tok = [st0.enter_context(nc.sbuf_tensor(PFX + f"xn_tok{i}", [128, D], BF16)) for i in range(2)]
            xsq = st0.enter_context(nc.sbuf_tensor(PFX + "xsq", [128, D], BF16))
            stat = [st0.enter_context(nc.sbuf_tensor(PFX + f"stat{i}", [128, 8], F32)) for i in range(2)]
            xT_stage = st0.enter_context(nc.sbuf_tensor(PFX + "xT_st", [128, 32, 128], F32))
            psB = [st0.enter_context(nc.psum_tensor(PFX + f"psB{i}", [128, 8, 128], BF16)) for i in range(2)]
            psF = [st0.enter_context(nc.psum_tensor(PFX + f"psF{i}", [128, 4, 128], F32)) for i in range(2)]

            @block.sync
            def _(s):
                ctx.gate(s)
                for i in range(NT):
                    b = i % 2
                    if i >= 2:
                        s.wait_ge(sem["tf"], (i - 1) * 8)
                    s.dma_start(out=x_tok[b][:], in_=x[bass.ts(i, 128), :]).then_inc(sem[f"ld{b}"], 16)

            @block.vector
            def _(v):
                ctx.gate(v)
                for i in range(NT):
                    b = i % 2
                    v.wait_ge(sem[f"ld{b}"], 16 * (i // 2 + 1))
                    if i >= 2:
                        v.wait_ge(sem["srt"], i - 1)
                    vbar(v, v.reduce_sum(stat[b][:, 0:1], x_tok[b][:], axis=AX.X))
                    v.wait_ge(sem["sq"], i + 1)
                    vbar(v, v.reduce_sum(stat[b][:, 1:2], xsq[:], axis=AX.X))
                    vbar(v, v.tensor_scalar_mul(stat[b][:, 2:3], stat[b][:, 0:1], 1.0 / D))
                    vbar(v, v.tensor_scalar(out=stat[b][:, 7:8], in0=stat[b][:, 2:3],
                                            scalar1=stat[b][:, 2:3], scalar2=-1e-5,
                                            op0=ALU.mult, op1=ALU.add))
                    v.scalar_tensor_tensor(out=stat[b][:, 3:4], in0=stat[b][:, 1:2],
                                           scalar=1.0 / D, in1=stat[b][:, 7:8],
                                           op0=ALU.mult, op1=ALU.subtract).then_inc(sem["var"], 1)
                    v.wait_ge(sem["srt"], i + 1)
                    vbar(v, v.scalar_tensor_tensor(out=stat[b][:, 6:7], in0=stat[b][:, 2:3],
                                                   scalar=-1.0, in1=stat[b][:, 5:6],
                                                   op0=ALU.mult, op1=ALU.mult))
                    v.tensor_scalar(out=xn_tok[b][:], in0=x_tok[b][:],
                                    scalar1=stat[b][:, 5:6], scalar2=stat[b][:, 6:7],
                                    op0=ALU.mult, op1=ALU.add).then_inc(sem["xnr"], 1)
                    for gi in range(8):
                        v.wait_ge(sem["tf"], i * 8 + gi + 1)
                        if gi == 0 and i >= 1:
                            v.wait_ge(sem["xto"], 16 * i)
                        v.tensor_copy(xT_stage[:, bass.ts(gi, 4), :],
                                      psF[gi % 2][:]).then_inc(sem["ef"], 1)

            @block.scalar
            def _(sc):
                ctx.gate(sc)
                for i in range(NT):
                    b = i % 2
                    sc.wait_ge(sem[f"ld{b}"], 16 * (i // 2 + 1))
                    if i >= 1:
                        sc.wait_ge(sem["var"], i)
                    sc.activation(out=xsq[:], in_=x_tok[b][:], func=AF.Square).then_inc(sem["sq"], 1)
                    sc.wait_ge(sem["var"], i + 1)
                    sc.wait_ge(sem["idn"], 3)
                    sbar(sc, sc.activation(out=stat[b][:, 4:5], in_=stat[b][:, 3:4], func=AF.Ln))
                    sc.activation(out=stat[b][:, 5:6], in_=stat[b][:, 4:5], func=AF.Exp,
                                  scale=cpool[:, 1:2]).then_inc(sem["srt"], 1)
                    for gi in range(4):
                        sc.wait_ge(sem["tb"], i * 4 + gi + 1)
                        sc.copy(xn_sb[:, bass.ts(gi, 8), bass.ts(i, 128)],
                                psB[gi % 2][:]).then_inc(sem["eb"], 1)

            @block.tensor
            def _(t):
                ctx.gate(t)
                t.wait_ge(sem["idn"], 3)
                for i in range(NT):
                    b = i % 2
                    t.wait_ge(sem["xnr"], i + 1)
                    for gi in range(4):
                        if i * 4 + gi >= 2:
                            t.wait_ge(sem["eb"], i * 4 + gi - 1)
                        for j in range(8):
                            ch = gi * 8 + j
                            ti = t.transpose(psB[gi % 2][:, j, :],
                                             xn_tok[b][:, bass.ts(ch, 128)], identB[:])
                        ti.then_inc(sem["tb"], 1)
                    for gi in range(8):
                        if i * 8 + gi >= 2:
                            t.wait_ge(sem["ef"], i * 8 + gi - 1)
                        for j in range(4):
                            ch = gi * 4 + j
                            ti = t.transpose(psF[gi % 2][:, j, :],
                                             x_tok[b][:, bass.ts(ch, 128)], identF[:])
                        ti.then_inc(sem["tf"], 1)

            @block.gpsimd
            def _(g):
                ctx.gate(g)
                for i in range(NT):
                    g.wait_ge(sem["ef"], i * 8 + 8)
                    g.dma_start(out=xT.rearrange("(ch p) t -> p ch t", p=128)[:, :, bass.ts(i, 128)],
                                in_=xT_stage[:]).then_inc(sem["xto"], 16)

            @block.sync
            def _(s):
                ctx.gate(s)
                s.wait_ge(sem["eb"], NT * 4)
                s.dma_start(out=xn.rearrange("(ch p) t -> p ch t", p=128),
                            in_=xn_sb[:]).then_inc(sem["xno"], 16)

        with contextlib.ExitStack() as st1:
            wbuf = [st1.enter_context(nc.sbuf_tensor(PFX + f"wbuf{i}", [128, 32, 128], BF16)) for i in range(2)]
            qstage = [st1.enter_context(nc.sbuf_tensor(PFX + f"qstage{i}", [128, T], BF16)) for i in range(2)]
            bv_sb = st1.enter_context(nc.sbuf_tensor(PFX + "bv_sb", [128, 8], F32))
            psQ = [st1.enter_context(nc.psum_tensor(PFX + f"psQ{i}", [128, 4, 512], F32)) for i in range(2)]

            jobs = ([(wqT, o, QSCALE, None, q0) for o in range(32)]
                    + [(wkT, o, 1.0, None, k0) for o in range(8)]
                    + [(wvT, o, 1.0, "bv", v0) for o in range(8)])

            @block.sync
            def _(s):
                ctx.gate(s)
                s.wait_ge(sem["xno"], 16)
                s.wait_ge(sem["xto"], 16 * NT)
                s.dma_start(out=bv_sb[:], in_=bv).then_inc(sem["bvl"], 16)

            @block.gpsimd
            def _(g):
                ctx.gate(g)
                g.wait_ge(sem["xno"], 16)
                g.wait_ge(sem["xto"], 16 * NT)
                for j, (wT, o, scl, bias, dest) in enumerate(jobs):
                    if j >= 2:
                        g.wait_ge(sem["mmq"], j - 1)
                    g.dma_start(out=wbuf[j % 2][:],
                                in_=wT.rearrange("(ch p) o -> p ch o", p=128)[:, :, bass.ts(o, 128)]
                                ).then_inc(sem[f"wld{j % 2}"], 16)

            @block.tensor
            def _(t):
                ctx.gate(t)
                t.wait_ge(sem["bvl"], 16)
                for j, (wT, o, scl, bias, dest) in enumerate(jobs):
                    t.wait_ge(sem[f"wld{j % 2}"], 16 * (j // 2 + 1))
                    if j >= 2:
                        t.wait_ge(sem["evq"], (j - 1) * 4)
                    for tc in range(4):
                        for ch in range(32):
                            mi = t.matmul(psQ[j % 2][:, tc, :], wbuf[j % 2][:, ch, :],
                                          xn_sb[:, ch, bass.ts(tc, 512)],
                                          start=(ch == 0), stop=(ch == 31))
                    mi.then_inc(sem["mmq"], 1)

            @block.vector
            def _(v):
                ctx.gate(v)
                for j, (wT, o, scl, bias, dest) in enumerate(jobs):
                    v.wait_ge(sem["mmq"], j + 1)
                    if j >= 2:
                        v.wait_ge(sem[f"qout{j % 2}"], 16 * (j // 2))
                    for tc in range(4):
                        dst = qstage[j % 2][:, bass.ts(tc, 512)]
                        src = psQ[j % 2][:, tc, :]
                        if bias == "bv":
                            v.tensor_scalar_add(dst, src, bv_sb[:, o:o + 1]).then_inc(sem["evq"], 1)
                        elif scl != 1.0:
                            v.tensor_scalar_mul(dst, src, scl).then_inc(sem["evq"], 1)
                        else:
                            v.tensor_copy(dst, src).then_inc(sem["evq"], 1)

            @block.sync
            def _(s):
                ctx.gate(s)
                for j, (wT, o, scl, bias, dest) in enumerate(jobs):
                    s.wait_ge(sem["evq"], (j + 1) * 4)
                    s.dma_start(out=dest[bass.ts(o, 128), :],
                                in_=qstage[j % 2][:]).then_inc(sem[f"qout{j % 2}"], 16)
                s.wait_ge(sem["qout0"], 16 * ((len(jobs) + 1) // 2))
                s.wait_ge(sem["qout1"], 16 * (len(jobs) // 2))
                s.wait_ge(sem["xno"], 16)




    return sem


def build_l0_pre():
    return _mk_standalone("l0_pre", [("x", [T, D], F32, "ExternalInput"), ("wqT", [D, D], F32, "ExternalInput"),
      ("wkT", [D, 1024], F32, "ExternalInput"), ("wvT", [D, 1024], F32, "ExternalInput"),
      ("bv", [128, 8], F32, "ExternalInput"), ("xT", [D, T], F32, "ExternalOutput"),
      ("xn", [D, T], BF16, "ExternalOutput"), ("q0", [D, T], BF16, "ExternalOutput"),
      ("k0", [1024, T], BF16, "ExternalOutput"), ("v0", [1024, T], BF16, "ExternalOutput")], emit_l0_pre)


def emit_l0_attn(ctx, io, NH=32):
    nc = ctx.nc
    block = ctx.block
    PFX = "l0_attn:"
    NKV = 8
    q0 = io["q0"]
    k0 = io["k0"]
    v0 = io["v0"]
    ctx0 = io["ctx0"]
    pr = io["pr"]
    with contextlib.ExitStack() as top:
        K_sb = top.enter_context(nc.sbuf_tensor(PFX + "K_sb", [128, NKV, T], BF16))
        Vt_sb = top.enter_context(nc.sbuf_tensor(PFX + "Vt_sb", [128, NKV, 16, 128], BF16))
        qh_sb = [top.enter_context(nc.sbuf_tensor(PFX + f"qh{i}", [128, T], BF16)) for i in range(2)]
        probs_sb = [top.enter_context(nc.sbuf_tensor(PFX + f"pb{i}", [128, T], BF16)) for i in range(2)]
        pT_sb = [top.enter_context(nc.sbuf_tensor(PFX + f"pT{i}", [128, T], BF16)) for i in range(2)]
        ctx_st = top.enter_context(nc.sbuf_tensor(PFX + "ctx_st", [128, T], BF16))
        mx = [top.enter_context(nc.sbuf_tensor(PFX + f"mx{i}", [128, 2], F32)) for i in range(2)]
        sums = [top.enter_context(nc.sbuf_tensor(PFX + f"sums{i}", [128, 2], F32)) for i in range(2)]
        ps = [top.enter_context(nc.psum_tensor(PFX + f"ps{i}", [128, 4, 512], F32)) for i in range(2)]
        sem = {n: nc.alloc_semaphore(name="l0_attn:" + n) for n in
               ["k", "vt", "qh0", "qh1", "sc", "mxs", "ex", "nm",
                "pw0", "pw1", "pt0", "pt1", "cx", "cev", "cout", "vss"]}
        vss = [0]

        def vbar(v, inst):
            vss[0] += 1
            inst.then_inc(sem["vss"], 1)
            v.wait_ge(sem["vss"], vss[0])

        @block.sync
        def _(s):
            ctx.gate(s)
            s.dma_start(out=K_sb[:], in_=k0.rearrange("(h p) t -> p h t", p=128)).then_inc(sem["k"], 16)
            for kvh in range(NKV):
                for j in range(16):
                    s.dma_start(out=Vt_sb[:, kvh, j, :],
                                in_=v0[bass.ts(kvh, 128), bass.ts(j, 128)],
                                transpose=True).then_inc(sem["vt"], 16)

        @block.gpsimd
        def _(g):
            ctx.gate(g)
            def ctx_write(hh):
                g.wait_ge(sem["cev"], hh + 1)
                g.dma_start(out=ctx0[bass.ts(hh, 128), :],
                            in_=ctx_st[:]).then_inc(sem["cout"], 16)

            for h in range(NH):
                if h >= 3:
                    ctx_write(h - 3)
                if h >= 2:
                    g.wait_ge(sem["sc"], 16 * (h - 1))
                g.dma_start(out=qh_sb[h % 2][:],
                            in_=q0[bass.ts(h, 128), :]).then_inc(sem[f"qh{h % 2}"], 16)
            for hh in range(max(0, NH - 3), NH):
                ctx_write(hh)
            g.wait_ge(sem["cout"], 16 * NH)

        @block.tensor
        def _(t):
            ctx.gate(t)
            t.wait_ge(sem["k"], 16)
            t.wait_ge(sem["vt"], 16 * NKV * 16)
            for h in range(NH):
                kvh = h // 4
                t.wait_ge(sem[f"qh{h % 2}"], 16 * (h // 2 + 1))
                for qt in range(16):
                    m = 16 * h + qt
                    if m >= 2:
                        t.wait_ge(sem["ex"], m - 1)
                    if qt == 1 and h >= 1:
                        t.wait_ge(sem["cev"], h)
                    for tc in range(4):
                        mi = t.matmul(ps[m % 2][:, tc, :],
                                      qh_sb[h % 2][:, bass.ts(qt, 128)],
                                      K_sb[:, kvh, bass.ts(tc, 512)],
                                      start=True, stop=True)
                    mi.then_inc(sem["sc"], 1)
                for kc in range(16):
                    n = 16 * h + kc
                    t.wait_ge(sem[f"pt{n % 2}"], 16 * (n // 2 + 1))
                    for tc in range(4):
                        mi = t.matmul(ps[1][:, tc, :],
                                      Vt_sb[:, kvh, kc, :],
                                      pT_sb[n % 2][:, bass.ts(tc, 512)],
                                      start=(kc == 0), stop=(kc == 15))
                    mi.then_inc(sem["cx"], 1)

        @block.vector
        def _(v):
            ctx.gate(v)
            for h in range(NH):
                for qt in range(16):
                    m = 16 * h + qt
                    b = m % 2
                    v.wait_ge(sem["sc"], m + 1)
                    vbar(v, v.reduce_max(mx[b][:, 0:1], ps[b][:], axis=AX.XY))
                    v.tensor_scalar_mul(mx[b][:, 1:2], mx[b][:, 0:1], -1.0).then_inc(sem["mxs"], 1)
                    v.wait_ge(sem["ex"], m + 1)
                    vbar(v, v.reciprocal(sums[b][:, 1:2], sums[b][:, 0:1]))
                    v.tensor_scalar_mul(probs_sb[b][:], probs_sb[b][:],
                                        sums[b][:, 1:2]).then_inc(sem["nm"], 1)

        @block.scalar
        def _(sc):
            ctx.gate(sc)
            for h in range(NH):
                for qt in range(16):
                    m = 16 * h + qt
                    b = m % 2
                    sc.wait_ge(sem["mxs"], m + 1)
                    if m >= 2:
                        sc.wait_ge(sem[f"pw{m % 2}"], 16 * (m // 2))
                    sc.activation(out=probs_sb[b][:], in_=ps[b][:], func=AF.Exp,
                                  bias=mx[b][:, 1:2],
                                  accum_out=sums[b][:, 0:1]).then_inc(sem["ex"], 1)
                for kc in range(16):
                    n = 16 * h + kc
                    sc.wait_ge(sem["pw0"], 16 * 8 * (h + 1))
                    sc.wait_ge(sem["pw1"], 16 * 8 * (h + 1))
                    if n >= 2:
                        sc.wait_ge(sem["cx"], n - 1)
                    sc.dma_start(out=pT_sb[n % 2][:], in_=pr[h % 2][:, bass.ts(kc, 128)],
                                 transpose=True).then_inc(sem[f"pt{n % 2}"], 16)
                sc.wait_ge(sem["cx"], 16 * (h + 1))
                if h >= 1:
                    sc.wait_ge(sem["cout"], 16 * h)
                sc.copy(ctx_st[:], ps[1][:]).then_inc(sem["cev"], 1)

        @block.sync
        def _(s):
            ctx.gate(s)
            for h in range(NH):
                for qt in range(16):
                    m = 16 * h + qt
                    s.wait_ge(sem["nm"], m + 1)
                    if qt == 0 and h >= 2:
                        s.wait_ge(sem["cx"], 16 * (h - 1))
                    s.dma_start(out=pr[h % 2, bass.ts(qt, 128), :],
                                in_=probs_sb[m % 2][:]).then_inc(sem[f"pw{m % 2}"], 16)




    return sem


def build_l0_attn(NH=32):
    return _mk_standalone("l0_attn", [("q0", [D, T], BF16, "ExternalInput"), ("k0", [1024, T], BF16, "ExternalInput"),
      ("v0", [1024, T], BF16, "ExternalInput"), ("ctx0", [D, T], BF16, "ExternalOutput"),
      ("pr", [2, T, T], BF16, "Internal")], emit_l0_attn, extra_args={"NH": NH})


def emit_l0_ffn1(ctx, io):
    nc = ctx.nc
    block = ctx.block
    PFX = "l0_ffn1:"
    xn = io["xn"]
    gqT = io["gqT"]
    uqT = io["uqT"]
    Sg = io["Sg"]
    Su = io["Su"]
    zgT = io["zgT"]
    zuT = io["zuT"]
    h0 = io["h0"]
    sigd = io["sigd"]
    NCH = 11
    with contextlib.ExitStack() as top:
        xn_sb = top.enter_context(nc.sbuf_tensor(PFX + "xn_sb", [128, 32, T], BF16))
        identB = top.enter_context(nc.sbuf_tensor(PFX + "identB", [128, 128], BF16))
        ones1 = top.enter_context(nc.sbuf_tensor(PFX + "ones1", [128, 1], BF16))
        words = {m: top.enter_context(nc.sbuf_tensor(PFX + f"words_{m}", [128, D], I32)) for m in "gu"}
        nib = top.enter_context(nc.sbuf_tensor(PFX + "nib", [128, D // 2], I32))
        wsc = top.enter_context(nc.sbuf_tensor(PFX + "wsc", [128, D // 2], BF16))
        W_sb = {m: top.enter_context(nc.sbuf_tensor(PFX + f"W_{m}", [128, 32, 128], BF16)) for m in "gu"}
        S_sb = {m: top.enter_context(nc.sbuf_tensor(PFX + f"S_{m}", [128, 8, 32], F32)) for m in "gu"}
        z_sb = {m: top.enter_context(nc.sbuf_tensor(PFX + f"z_{m}", [128, 32], I32)) for m in "gu"}
        znib = top.enter_context(nc.sbuf_tensor(PFX + "znib", [128, 32], I32))
        nzs = top.enter_context(nc.sbuf_tensor(PFX + "nzs", [128, 32], F32))
        nzs_bf = top.enter_context(nc.sbuf_tensor(PFX + "nzs_bf", [128, 32], BF16))
        nzsT = {m: top.enter_context(nc.sbuf_tensor(PFX + f"nzsT_{m}", [32, 128], BF16)) for m in "gu"}
        sig_sb = top.enter_context(nc.sbuf_tensor(PFX + "sig_sb", [32, T], BF16))
        sig_row = top.enter_context(nc.sbuf_tensor(PFX + "sig_row", [1, 512], F32))
        sg_sb = top.enter_context(nc.sbuf_tensor(PFX + "sg_sb", [128, 512], BF16))
        h_st = [top.enter_context(nc.sbuf_tensor(PFX + f"h_st{i}", [128, 2, 512], BF16)) for i in range(2)]

        psW = [top.enter_context(nc.psum_tensor(PFX + f"psW{i}", [128, 8, 128], BF16)) for i in range(2)]
        psZ = top.enter_context(nc.psum_tensor(PFX + "psZ", [32, 128], BF16))
        psSig = top.enter_context(nc.psum_tensor(PFX + "psSig", [1, 512], F32))
        gps = [top.enter_context(nc.psum_tensor(PFX + f"gps{i}", [128, 512], F32)) for i in range(2)]
        ups = [top.enter_context(nc.psum_tensor(PFX + f"ups{i}", [128, 512], F32)) for i in range(2)]

        sem = {n: nc.alloc_semaphore(name="l0_ffn1:" + n) for n in
               ["xnl", "idn", "wldg", "wldu", "zl", "sgm", "sge", "sgw", "sgl",
                "tts", "wtr", "wev", "ztr", "zev",
                "mmg", "mmu", "sil", "hml", "hout0", "hout1", "vss"]}
        vss = [0]
        sss = [0]

        def vbar(v, inst):
            vss[0] += 1
            inst.then_inc(sem["vss"], 1)
            v.wait_ge(sem["vss"], vss[0])

        slices = [(ch, c) for ch in range(NCH) for c in range(8)]
        WID = {ch: (96 if ch == NCH - 1 else 128) for ch in range(NCH)}

        @block.sync
        def _(s):
            ctx.gate(s)
            s.dma_start(out=xn_sb[:], in_=xn.rearrange("(ch p) t -> p ch t", p=128)).then_inc(sem["xnl"], 16)

        @block.gpsimd
        def _(g):
            ctx.gate(g)
            g.memset(identB[:], 1.0).then_inc(sem["idn"], 1)
            g.memset(ones1[:], 1.0).then_inc(sem["idn"], 1)
            g.wait_ge(sem["idn"], 2)
            g.affine_select(out=identB[:], in_=identB[:], compare_op=ALU.is_equal,
                            fill=0.0, base=0, pattern=[[-1, 128]],
                            channel_multiplier=1).then_inc(sem["idn"], 1)

        @block.tensor
        def _(t):
            ctx.gate(t)
            t.wait_ge(sem["xnl"], 16)
            t.wait_ge(sem["idn"], 3)
            for g in range(32):
                for tc in range(4):
                    if 4 * g + tc >= 1:
                        t.wait_ge(sem["sge"], 4 * g + tc)
                    t.matmul(psSig[:], ones1[:], xn_sb[:, g, bass.ts(tc, 512)],
                             start=True, stop=True).then_inc(sem["sgm"], 1)

        @block.scalar
        def _(sc):
            ctx.gate(sc)
            for g in range(32):
                for tc in range(4):
                    sc.wait_ge(sem["sgm"], 4 * g + tc + 1)
                    if 4 * g + tc >= 1:
                        sc.wait_ge(sem["sgw"], 16 * (4 * g + tc))
                    sc.copy(sig_row[:], psSig[:]).then_inc(sem["sge"], 1)

        @block.sync
        def _(s):
            ctx.gate(s)
            for g in range(32):
                for tc in range(4):
                    s.wait_ge(sem["sge"], 4 * g + tc + 1)
                    s.dma_start(out=sigd[g:g + 1, bass.ts(tc, 512)],
                                in_=sig_row[:]).then_inc(sem["sgw"], 16)

        @block.gpsimd
        def _(g):
            ctx.gate(g)
            g.wait_ge(sem["sgw"], 16 * 128)
            g.dma_start(out=sig_sb[:], in_=sigd).then_inc(sem["sgl"], 16)


        @block.sync
        def _(s):
            ctx.gate(s)
            for ch in range(NCH):
                if ch >= 1:
                    s.wait_ge(sem["zev"], 16 * ch)
                for m, zT, Sd in (("g", zgT, Sg), ("u", zuT, Su)):
                    s.dma_start(out=z_sb[m][:], in_=zT[bass.ts(ch, 128), :]).then_inc(sem["zl"], 16)
                    s.dma_start(out=S_sb[m][:], in_=Sd[ch]).then_inc(sem["zl"], 16)

        def emit_hmul(v, si):
            ch, c = slices[si]
            w = WID[ch]
            for tc in range(4):
                u = 4 * si + tc
                v.wait_ge(sem["sil"], u + 1)
                v.wait_ge(sem["mmu"], u + 1)
                if tc == 0 and si >= 2:
                    v.wait_ge(sem[f"hout{si % 2}"], 32 * (si // 2))
                if tc == 2:
                    v.wait_ge(sem[f"hout{si % 2}"], 32 * (si // 2) + 16)
                v.tensor_mul(h_st[si % 2][0:w, tc % 2, :], sg_sb[0:w, :],
                             ups[tc % 2][0:w, :]).then_inc(sem["hml"], 1)

        @block.vector
        def _(v):
            ctx.gate(v)
            for si, (ch, c) in enumerate(slices):
                sh = 4 * AWQ_ORDER[c]
                if si >= 1:
                    emit_hmul(v, si - 1)
                for mi, m in enumerate(("g", "u")):
                    k = 2 * si + mi
                    v.wait_ge(sem["wldg" if m == "g" else "wldu"], 16 * (ch + 1))
                    v.wait_ge(sem["zl"], 64 * (ch + 1))
                    for half in range(2):
                        if 2 * k + half >= 1:
                            v.wait_ge(sem["tts"], 2 * k + half)
                        vbar(v, v.tensor_scalar(out=nib[:], in0=words[m][:, bass.ts(half, D // 2)],
                                                scalar1=sh, scalar2=15,
                                                op0=ALU.logical_shift_right,
                                                op1=ALU.bitwise_and))
                        v.wait_ge(sem["wtr"], 4 * k + 2 * half)
                        v.tensor_tensor(wsc[:].rearrange("p (a b) -> p a b", b=128),
                                        nib[:].rearrange("p (a b) -> p a b", b=128),
                                        S_sb[m][:, c, 16 * half:16 * half + 16][:, :, None].broadcast_to([128, 16, 128]),
                                        ALU.mult).then_inc(sem["tts"], 1)
                    vbar(v, v.tensor_scalar(out=znib[:], in0=z_sb[m][:], scalar1=sh,
                                            scalar2=15, op0=ALU.logical_shift_right,
                                            op1=ALU.bitwise_and))
                    vbar(v, v.scalar_tensor_tensor(out=nzs[:], in0=znib[:], scalar=-1.0,
                                                   in1=S_sb[m][:, c, :], op0=ALU.mult,
                                                   op1=ALU.mult))
                    if k >= 1:
                        v.wait_ge(sem["ztr"], k)
                    vz = v.tensor_copy(nzs_bf[:], nzs[:])
                    vz.then_inc(sem["vss"], 1); vss[0] += 1
                    v.wait_ge(sem["vss"], vss[0])
            emit_hmul(v, len(slices) - 1)

        @block.tensor
        def _(t):
            ctx.gate(t)
            for si, (ch, c) in enumerate(slices):
                w = WID[ch]
                for mi, m in enumerate(("g", "u")):
                    k = 2 * si + mi
                    for half in range(2):
                        t.wait_ge(sem["tts"], 2 * k + half + 1)
                        for gi in range(2):
                            gidx = 4 * k + 2 * half + gi
                            if gidx >= 2:
                                t.wait_ge(sem["wev"], gidx - 1)
                            for j in range(8):
                                gg = gi * 8 + j
                                ti = t.transpose(psW[gidx % 2][:, j, 0:w],
                                                 wsc[0:w, bass.ts(gg, 128)], identB[0:w, 0:w])
                            ti.then_inc(sem["wtr"], 1)
                    t.wait_ge(sem["vss"], 5 * (k + 1))
                    if k >= 1:
                        t.wait_ge(sem["zev"], k)
                    t.transpose(psZ[:], nzs_bf[:], identB[:]).then_inc(sem["ztr"], 1)
                t.wait_ge(sem["wev"], (2 * si + 2) * 4)
                t.wait_ge(sem["zev"], 2 * si + 2)
                t.wait_ge(sem["sgl"], 16)
                for tc in range(4):
                    u = 4 * si + tc
                    if u >= 2:
                        t.wait_ge(sem["sil"], u - 1)
                        t.wait_ge(sem["hml"], u - 1)
                    for g in range(32):
                        t.matmul(gps[tc % 2][0:w, :], W_sb["g"][:, g, 0:w],
                                 xn_sb[:, g, bass.ts(tc, 512)],
                                 start=(g == 0), stop=False)
                    t.matmul(gps[tc % 2][0:w, :], nzsT["g"][:, 0:w],
                             sig_sb[:, bass.ts(tc, 512)],
                             start=False, stop=True).then_inc(sem["mmg"], 1)
                    for g in range(32):
                        t.matmul(ups[tc % 2][0:w, :], W_sb["u"][:, g, 0:w],
                                 xn_sb[:, g, bass.ts(tc, 512)],
                                 start=(g == 0), stop=False)
                    t.matmul(ups[tc % 2][0:w, :], nzsT["u"][:, 0:w],
                             sig_sb[:, bass.ts(tc, 512)],
                             start=False, stop=True).then_inc(sem["mmu"], 1)

        @block.scalar
        def _(sc):
            ctx.gate(sc)
            for si, (ch, c) in enumerate(slices):
                w = WID[ch]
                for mi, m in enumerate(("g", "u")):
                    k = 2 * si + mi
                    if si >= 1:
                        sc.wait_ge(sem["mmu"], 4 * si)
                    for half in range(2):
                        for gi in range(2):
                            gidx = 4 * k + 2 * half + gi
                            sc.wait_ge(sem["wtr"], gidx + 1)
                            sc.copy(W_sb[m][:, 16 * half + 8 * gi:16 * half + 8 * gi + 8, :],
                                    psW[gidx % 2][:]).then_inc(sem["wev"], 1)
                    sc.wait_ge(sem["ztr"], k + 1)
                    sc.copy(nzsT[m][:], psZ[:]).then_inc(sem["zev"], 1)
                for tc in range(4):
                    u = 4 * si + tc
                    sc.wait_ge(sem["mmg"], u + 1)
                    if u >= 1:
                        sc.wait_ge(sem["hml"], u)
                    sc.activation(out=sg_sb[0:w, :], in_=gps[tc % 2][0:w, :],
                                  func=AF.Silu).then_inc(sem["sil"], 1)

        @block.gpsimd
        def _(g):
            ctx.gate(g)
            hv = h0.rearrange("(nw c8) (th t2) -> nw c8 th t2", c8=8, t2=1024)
            for si, (ch, c) in enumerate(slices):
                w = WID[ch]
                if c == 0:
                    for m, qT, sname in (("g", gqT, "wldg"), ("u", uqT, "wldu")):
                        if ch >= 1:
                            g.wait_ge(sem["tts"], 32 * ch)
                        g.dma_start(out=words[m][:],
                                    in_=qT[bass.ts(ch, 128), :]).then_inc(sem[sname], 16)
                for half in range(2):
                    g.wait_ge(sem["hml"], 4 * si + 2 * (half + 1))
                    g.dma_start(out=hv[128 * ch:128 * ch + w, c, half, :],
                                in_=h_st[si % 2][0:w, :, :]).then_inc(sem[f"hout{si % 2}"], 16)
            g.wait_ge(sem["hout0"], 32 * ((len(slices) + 1) // 2))
            g.wait_ge(sem["hout1"], 32 * (len(slices) // 2))




    return sem


def build_l0_ffn1():
    return _mk_standalone("l0_ffn1", [("xn", [D, T], BF16, "ExternalInput"), ("gqT", [1408, D], I32, "ExternalInput"),
      ("uqT", [1408, D], I32, "ExternalInput"), ("Sg", [11, 128, 8, 32], F32, "ExternalInput"),
      ("Su", [11, 128, 8, 32], F32, "ExternalInput"), ("zgT", [1408, 32], I32, "ExternalInput"),
      ("zuT", [1408, 32], I32, "ExternalInput"), ("h0", [11008, T], BF16, "ExternalOutput"),
      ("sigd", [32, T], F32, "Internal")], emit_l0_ffn1)


def emit_l0_out(ctx, io):
    nc = ctx.nc
    block = ctx.block
    PFX = "l0_out:"
    h0 = io["h0"]
    ctx0 = io["ctx0"]
    dqT = io["dqT"]
    Sd = io["Sd"]
    zdT = io["zdT"]
    woTp = io["woTp"]
    xT = io["xT"]
    yA = io["yA"]
    wdq = io["wdq"]
    sigd = io["sigd"]
    NG = 86
    HALF = 43
    with contextlib.ExitStack() as top:
        identB = top.enter_context(nc.sbuf_tensor(PFX + "identB", [128, 128], BF16))
        ones1 = top.enter_context(nc.sbuf_tensor(PFX + "ones1", [128, 1], BF16))
        nzsT_d = top.enter_context(nc.sbuf_tensor(PFX + "nzsT_d", [86, 32, 128], BF16))
        sig_sb = top.enter_context(nc.sbuf_tensor(PFX + "sig_sb", [86, T], BF16))
        sem = {n: nc.alloc_semaphore(name="l0_out:" + n) for n in
               ["idn", "wldd", "zl", "tts", "wtr", "wev", "ztr", "zev", "wdw0", "wdw1",
                "sgh0", "sgh1", "sgm", "sge", "sgw", "sgl",
                "hld", "cld", "wdl0", "wdl1", "wol0", "wol1", "xtl0", "xtl1",
                "mm", "yev", "yout0", "yout1", "vss"]}
        vss = [0]

        def vbar(v, inst):
            vss[0] += 1
            inst.then_inc(sem["vss"], 1)
            v.wait_ge(sem["vss"], vss[0])

        @block.gpsimd
        def _(g):
            ctx.gate(g)
            g.memset(identB[:], 1.0).then_inc(sem["idn"], 1)
            g.memset(ones1[:], 1.0).then_inc(sem["idn"], 1)
            g.wait_ge(sem["idn"], 2)
            g.affine_select(out=identB[:], in_=identB[:], compare_op=ALU.is_equal,
                            fill=0.0, base=0, pattern=[[-1, 128]],
                            channel_multiplier=1).then_inc(sem["idn"], 1)

        with contextlib.ExitStack() as stA:
            words_d = stA.enter_context(nc.sbuf_tensor(PFX + "words_d", [128, 11008], I32))
            nib_d = stA.enter_context(nc.sbuf_tensor(PFX + "nib_d", [128, HALF * 128], I32))
            wsc_d = stA.enter_context(nc.sbuf_tensor(PFX + "wsc_d", [128, HALF * 128], BF16))
            S_sb = stA.enter_context(nc.sbuf_tensor(PFX + "Sd_sb", [128, 8, 86], F32))
            z_sb = stA.enter_context(nc.sbuf_tensor(PFX + "zd_sb", [128, 86], I32))
            znib = stA.enter_context(nc.sbuf_tensor(PFX + "znibd", [128, 86], I32))
            nzs = stA.enter_context(nc.sbuf_tensor(PFX + "nzsd", [128, 86], F32))
            nzs_bf = stA.enter_context(nc.sbuf_tensor(PFX + "nzsd_bf", [128, 86], BF16))
            wstage = [stA.enter_context(nc.sbuf_tensor(PFX + f"wstage{i}", [128, 8, 128], BF16)) for i in range(2)]
            psW = [stA.enter_context(nc.psum_tensor(PFX + f"psWd{i}", [128, 8, 128], BF16)) for i in range(2)]
            psZ = stA.enter_context(nc.psum_tensor(PFX + "psZd", [86, 128], BF16))

            GRP = [(gi * 8, min(8, HALF - gi * 8)) for gi in range((HALF + 7) // 8)]
            NGRP = len(GRP)

            @block.gpsimd
            def _(g):
                ctx.gate(g)
                for oc in range(4):
                    if oc >= 1:
                        g.wait_ge(sem["tts"], 16 * oc)
                    g.dma_start(out=words_d[:], in_=dqT[bass.ts(oc, 128), :]).then_inc(sem["wldd"], 16)


            @block.vector
            def _(v):
                ctx.gate(v)
                for oc in range(4):
                    for c in range(8):
                        k = 8 * oc + c
                        sh = 4 * AWQ_ORDER[c]
                        v.wait_ge(sem["wldd"], 16 * (oc + 1))
                        v.wait_ge(sem["zl"], 32 * (oc + 1))
                        for half in range(2):
                            if 2 * k + half >= 1:
                                v.wait_ge(sem["tts"], 2 * k + half)
                            vbar(v, v.tensor_scalar(out=nib_d[:], in0=words_d[:, bass.ts(half, HALF * 128)],
                                                    scalar1=sh, scalar2=15,
                                                    op0=ALU.logical_shift_right,
                                                    op1=ALU.bitwise_and))
                            v.wait_ge(sem["wtr"], NGRP * (2 * k + half))
                            v.tensor_tensor(wsc_d[:].rearrange("p (a b) -> p a b", b=128),
                                            nib_d[:].rearrange("p (a b) -> p a b", b=128),
                                            S_sb[:, c, HALF * half:HALF * (half + 1)][:, :, None].broadcast_to([128, HALF, 128]),
                                            ALU.mult).then_inc(sem["tts"], 1)
                        vbar(v, v.tensor_scalar(out=znib[:], in0=z_sb[:], scalar1=sh,
                                                scalar2=15, op0=ALU.logical_shift_right,
                                                op1=ALU.bitwise_and))
                        vbar(v, v.scalar_tensor_tensor(out=nzs[:], in0=znib[:], scalar=-1.0,
                                                       in1=S_sb[:, c, :], op0=ALU.mult,
                                                       op1=ALU.mult))
                        if k >= 1:
                            v.wait_ge(sem["ztr"], k)
                        vz = v.tensor_copy(nzs_bf[:], nzs[:])
                        vz.then_inc(sem["vss"], 1); vss[0] += 1
                        v.wait_ge(sem["vss"], vss[0])

            @block.tensor
            def _(t):
                ctx.gate(t)
                t.wait_ge(sem["idn"], 3)
                for oc in range(4):
                    for c in range(8):
                        k = 8 * oc + c
                        for half in range(2):
                            t.wait_ge(sem["tts"], 2 * k + half + 1)
                            for gi, (g0, gn) in enumerate(GRP):
                                gidx = NGRP * (2 * k + half) + gi
                                if gidx >= 2:
                                    t.wait_ge(sem["wev"], gidx - 1)
                                for j in range(gn):
                                    ti = t.transpose(psW[gidx % 2][:, j, :],
                                                     wsc_d[:, bass.ts(g0 + j, 128)], identB[:])
                                ti.then_inc(sem["wtr"], 1)
                        t.wait_ge(sem["vss"], 5 * (k + 1))
                        if k >= 1:
                            t.wait_ge(sem["zev"], k)
                        t.transpose(psZ[:], nzs_bf[:], identB[:]).then_inc(sem["ztr"], 1)

            @block.scalar
            def _(sc):
                ctx.gate(sc)
                for oc in range(4):
                    for c in range(8):
                        k = 8 * oc + c
                        for half in range(2):
                            for gi, (g0, gn) in enumerate(GRP):
                                gidx = NGRP * (2 * k + half) + gi
                                sc.wait_ge(sem["wtr"], gidx + 1)
                                if gidx >= 2:
                                    sc.wait_ge(sem[f"wdw{gidx % 2}"], 16 * (gidx // 2))
                                sc.copy(wstage[gidx % 2][:, 0:gn, :],
                                        psW[gidx % 2][:, 0:gn, :]).then_inc(sem["wev"], 1)
                        sc.wait_ge(sem["ztr"], k + 1)
                        sc.copy(nzsT_d[:, k, :], psZ[:]).then_inc(sem["zev"], 1)

            @block.sync
            def _(s):
                ctx.gate(s)
                for oc in range(4):
                    if oc >= 1:
                        s.wait_ge(sem["zev"], 8 * oc)
                    s.dma_start(out=z_sb[:], in_=zdT[bass.ts(oc, 128), :]).then_inc(sem["zl"], 16)
                    s.dma_start(out=S_sb[:], in_=Sd[oc]).then_inc(sem["zl"], 16)
                    for c in range(8):
                        k = 8 * oc + c
                        for half in range(2):
                            for gi, (g0, gn) in enumerate(GRP):
                                gidx = NGRP * (2 * k + half) + gi
                                s.wait_ge(sem["wev"], gidx + 1)
                                gg = HALF * half + g0
                                s.dma_start(out=wdq[oc, c, gg:gg + gn].rearrange("g p o -> p g o"),
                                            in_=wstage[gidx % 2][:, 0:gn, :]).then_inc(sem[f"wdw{gidx % 2}"], 16)

        with contextlib.ExitStack() as stB:
            hh = [stB.enter_context(nc.sbuf_tensor(PFX + f"hh{i}", [128, T], BF16)) for i in range(2)]
            sig_row = stB.enter_context(nc.sbuf_tensor(PFX + "sig_rowd", [1, 512], F32))
            psSig = stB.enter_context(nc.psum_tensor(PFX + "psSigd", [1, 512], F32))

            @block.gpsimd
            def _(g):
                ctx.gate(g)
                g.wait_ge(sem["wdw0"], 16 * (4 * 8 * 2 * 6 // 2))
                g.wait_ge(sem["wdw1"], 16 * (4 * 8 * 2 * 6 // 2))
                for gg in range(NG):
                    if gg >= 2:
                        g.wait_ge(sem["sgm"], 4 * (gg - 1))
                    g.dma_start(out=hh[gg % 2][:], in_=h0[bass.ts(gg, 128), :]).then_inc(
                        sem[f"sgh{gg % 2}"], 16)

            @block.tensor
            def _(t):
                ctx.gate(t)
                for gg in range(NG):
                    t.wait_ge(sem[f"sgh{gg % 2}"], 16 * (gg // 2 + 1))
                    for tc in range(4):
                        if 4 * gg + tc >= 1:
                            t.wait_ge(sem["sge"], 4 * gg + tc)
                        t.matmul(psSig[:], ones1[:], hh[gg % 2][:, bass.ts(tc, 512)],
                                 start=True, stop=True).then_inc(sem["sgm"], 1)

            @block.scalar
            def _(sc):
                ctx.gate(sc)
                for gg in range(NG):
                    for tc in range(4):
                        sc.wait_ge(sem["sgm"], 4 * gg + tc + 1)
                        if 4 * gg + tc >= 1:
                            sc.wait_ge(sem["sgw"], 16 * (4 * gg + tc))
                        sc.copy(sig_row[:], psSig[:]).then_inc(sem["sge"], 1)

            @block.sync
            def _(s):
                ctx.gate(s)
                for gg in range(NG):
                    for tc in range(4):
                        s.wait_ge(sem["sge"], 4 * gg + tc + 1)
                        s.dma_start(out=sigd[gg:gg + 1, bass.ts(tc, 512)],
                                    in_=sig_row[:]).then_inc(sem["sgw"], 16)

            @block.gpsimd
            def _(g):
                ctx.gate(g)
                g.wait_ge(sem["sgw"], 16 * 4 * NG)
                g.dma_start(out=sig_sb[:], in_=sigd).then_inc(sem["sgl"], 16)

        with contextlib.ExitStack() as stC:
            h_all = stC.enter_context(nc.sbuf_tensor(PFX + "h_all", [128, NG, 512], BF16))
            ctx_tc = stC.enter_context(nc.sbuf_tensor(PFX + "ctx_tc", [128, 32, 512], BF16))
            wd_sb = [stC.enter_context(nc.sbuf_tensor(PFX + f"wd_sb{i}", [128, NG, 128], BF16)) for i in range(2)]
            wo_sb = [stC.enter_context(nc.sbuf_tensor(PFX + f"wo_sb{i}", [128, 32, 128], BF16)) for i in range(2)]
            xt_sb = [stC.enter_context(nc.sbuf_tensor(PFX + f"xt_sb{i}", [128, 512], F32)) for i in range(2)]
            y_sb = [stC.enter_context(nc.sbuf_tensor(PFX + f"y_sb{i}", [128, 512], F32)) for i in range(2)]
            psY = [stC.enter_context(nc.psum_tensor(PFX + f"psY{i}", [128, 512], F32)) for i in range(2)]

            xv = xT.rearrange("(ow c8) t -> ow c8 t", c8=8)
            yv = yA.rearrange("(ow c8) t -> ow c8 t", c8=8)

            @block.gpsimd
            def _(g):
                ctx.gate(g)
                g.wait_ge(sem["sgl"], 16)
                for tc in range(4):
                    if tc >= 1:
                        g.wait_ge(sem["mm"], 32 * tc)
                    g.dma_start(out=h_all[:], in_=h0.rearrange("(g p) t -> p g t", p=128)[:, :, bass.ts(tc, 512)]).then_inc(sem["hld"], 16)
                    g.dma_start(out=ctx_tc[:], in_=ctx0.rearrange("(g p) t -> p g t", p=128)[:, :, bass.ts(tc, 512)]).then_inc(sem["cld"], 16)
                    for so in range(32):
                        oc, c = so // 8, so % 8
                        j = 32 * tc + so
                        if j >= 2:
                            g.wait_ge(sem["mm"], j - 1)
                            g.wait_ge(sem["yev"], j - 1)
                        g.dma_start(out=wd_sb[so % 2][:],
                                    in_=wdq[oc, c].rearrange("g p o -> p g o")).then_inc(sem[f"wdl{so % 2}"], 16)
                        g.dma_start(out=wo_sb[so % 2][:],
                                    in_=woTp.rearrange("(ch p) a b o -> p ch a b o", p=128)[:, :, oc, c, :]).then_inc(sem[f"wol{so % 2}"], 16)
                        g.dma_start(out=xt_sb[so % 2][:],
                                    in_=xv[bass.ts(oc, 128), c, bass.ts(tc, 512)]).then_inc(sem[f"xtl{so % 2}"], 16)

            @block.tensor
            def _(t):
                ctx.gate(t)
                for tc in range(4):
                    t.wait_ge(sem["hld"], 16 * (tc + 1))
                    t.wait_ge(sem["cld"], 16 * (tc + 1))
                    for so in range(32):
                        j = 32 * tc + so
                        b = so % 2
                        t.wait_ge(sem[f"wdl{b}"], 16 * (j // 2 + 1))
                        t.wait_ge(sem[f"wol{b}"], 16 * (j // 2 + 1))
                        if j >= 2:
                            t.wait_ge(sem["yev"], j - 1)
                        for gg in range(NG):
                            t.matmul(psY[j % 2][:], wd_sb[b][:, gg, :],
                                     h_all[:, gg, :], start=(gg == 0), stop=False)
                        t.matmul(psY[j % 2][:], nzsT_d[:, so, :],
                                 sig_sb[:, bass.ts(tc, 512)], start=False, stop=False)
                        for ch in range(32):
                            ti = t.matmul(psY[j % 2][:], wo_sb[b][:, ch, :],
                                          ctx_tc[:, ch, :], start=False,
                                          stop=(ch == 31))
                        ti.then_inc(sem["mm"], 1)

            @block.vector
            def _(v):
                ctx.gate(v)
                for tc in range(4):
                    for so in range(32):
                        j = 32 * tc + so
                        b = so % 2
                        v.wait_ge(sem["mm"], j + 1)
                        v.wait_ge(sem[f"xtl{b}"], 16 * (j // 2 + 1))
                        if j >= 2:
                            v.wait_ge(sem[f"yout{j % 2}"], 16 * (j // 2))
                        v.tensor_add(y_sb[j % 2][:], psY[j % 2][:],
                                     xt_sb[b][:]).then_inc(sem["yev"], 1)

            @block.sync
            def _(s):
                ctx.gate(s)
                for tc in range(4):
                    for so in range(32):
                        oc, c = so // 8, so % 8
                        j = 32 * tc + so
                        s.wait_ge(sem["yev"], j + 1)
                        s.dma_start(out=yv[bass.ts(oc, 128), c, bass.ts(tc, 512)],
                                    in_=y_sb[j % 2][:]).then_inc(sem[f"yout{j % 2}"], 16)
                s.wait_ge(sem["yout0"], 16 * 64)
                s.wait_ge(sem["yout1"], 16 * 64)




    return sem


def build_l0_out():
    return _mk_standalone("l0_out", [("h0", [11008, T], BF16, "ExternalInput"), ("ctx0", [D, T], BF16, "ExternalInput"),
      ("dqT", [512, 11008], I32, "ExternalInput"), ("Sd", [4, 128, 8, 86], F32, "ExternalInput"),
      ("zdT", [512, 86], I32, "ExternalInput"), ("woTp", [D, 4, 8, 128], F32, "ExternalInput"),
      ("xT", [D, T], F32, "ExternalInput"), ("yA", [D, T], F32, "ExternalOutput"),
      ("wdq", [4, 8, 86, 128, 128], BF16, "Internal"), ("sigd", [86, T], F32, "Internal")], emit_l0_out)


def emit_l1_pre(ctx, io):
    nc = ctx.nc
    block = ctx.block
    PFX = "l1_pre:"
    yA = io["yA"]
    wqkvT = io["wqkvT"]
    bqkv = io["bqkv"]
    yn = io["yn"]
    q1 = io["q1"]
    k1 = io["k1"]
    v1 = io["v1"]
    with contextlib.ExitStack() as top:
        yn_sb = top.enter_context(nc.sbuf_tensor(PFX + "yn_sb", [128, 32, T], BF16))
        identB = top.enter_context(nc.sbuf_tensor(PFX + "identB", [128, 128], BF16))
        identF = top.enter_context(nc.sbuf_tensor(PFX + "identF", [128, 128], F32))
        cpool = top.enter_context(nc.sbuf_tensor(PFX + "cpool", [128, 2], F32))
        sem = {n: nc.alloc_semaphore(name="l1_pre:" + n) for n in
               ["ld0", "ld1", "ty", "ey", "sq", "var", "srt", "xnr", "tb", "eb",
                "yno", "idn", "bvl", "wld0", "wld1", "mmq", "evq",
                "qout0", "qout1", "vss", "sss"]}
        vss = [0]
        sss = [0]

        def vbar(v, inst):
            vss[0] += 1
            inst.then_inc(sem["vss"], 1)
            v.wait_ge(sem["vss"], vss[0])

        def sbar(sc, inst):
            sss[0] += 1
            inst.then_inc(sem["sss"], 1)
            sc.wait_ge(sem["sss"], sss[0])

        @block.gpsimd
        def _(g):
            ctx.gate(g)
            g.memset(identB[:], 1.0).then_inc(sem["idn"], 1)
            g.memset(identF[:], 1.0).then_inc(sem["idn"], 1)
            g.wait_ge(sem["idn"], 2)
            g.affine_select(out=identB[:], in_=identB[:], compare_op=ALU.is_equal,
                            fill=0.0, base=0, pattern=[[-1, 128]], channel_multiplier=1)
            g.affine_select(out=identF[:], in_=identF[:], compare_op=ALU.is_equal,
                            fill=0.0, base=0, pattern=[[-1, 128]],
                            channel_multiplier=1)
            g.memset(cpool[:, 1:2], -0.5).then_inc(sem["idn"], 1)

        NT = 16
        with contextlib.ExitStack() as st0:
            ybuf = [st0.enter_context(nc.sbuf_tensor(PFX + f"ybuf{i}", [128, 32, 128], F32)) for i in range(2)]
            y_tok = st0.enter_context(nc.sbuf_tensor(PFX + "y_tok", [128, D], F32))
            yn_tok = st0.enter_context(nc.sbuf_tensor(PFX + "yn_tok", [128, D], BF16))
            xsq = st0.enter_context(nc.sbuf_tensor(PFX + "xsq", [128, D], BF16))
            stat = [st0.enter_context(nc.sbuf_tensor(PFX + f"stat{i}", [128, 8], F32)) for i in range(2)]
            psF = [st0.enter_context(nc.psum_tensor(PFX + f"psF{i}", [128, 4, 128], F32)) for i in range(2)]
            psB = [st0.enter_context(nc.psum_tensor(PFX + f"psB{i}", [128, 8, 128], BF16)) for i in range(2)]

            yv = yA.rearrange("(ch p) t -> p ch t", p=128)

            @block.sync
            def _(s):
                ctx.gate(s)
                for i in range(NT):
                    b = i % 2
                    if i >= 2:
                        s.wait_ge(sem["ty"], (i - 1) * 8)
                    s.dma_start(out=ybuf[b][:], in_=yv[:, :, bass.ts(i, 128)]).then_inc(sem[f"ld{b}"], 16)

            @block.tensor
            def _(t):
                ctx.gate(t)
                t.wait_ge(sem["idn"], 3)
                for i in range(NT):
                    b = i % 2
                    t.wait_ge(sem[f"ld{b}"], 16 * (i // 2 + 1))
                    for gi in range(8):
                        if i * 8 + gi >= 2:
                            t.wait_ge(sem["ey"], i * 8 + gi - 1)
                        for j in range(4):
                            ch = gi * 4 + j
                            ti = t.transpose(psF[gi % 2][:, j, :],
                                             ybuf[b][:, ch, :], identF[:])
                        ti.then_inc(sem["ty"], 1)
                    t.wait_ge(sem["xnr"], i + 1)
                    for gi in range(4):
                        if i * 4 + gi >= 2:
                            t.wait_ge(sem["eb"], i * 4 + gi - 1)
                        for j in range(8):
                            ch = gi * 8 + j
                            ti = t.transpose(psB[gi % 2][:, j, :],
                                             yn_tok[:, bass.ts(ch, 128)], identB[:])
                        ti.then_inc(sem["tb"], 1)

            @block.vector
            def _(v):
                ctx.gate(v)
                for i in range(NT):
                    b = i % 2
                    for gi in range(8):
                        v.wait_ge(sem["ty"], i * 8 + gi + 1)
                        if gi == 0 and i >= 1:
                            v.wait_ge(sem["xnr"], i)
                        v.tensor_copy(y_tok[:, 512 * gi:512 * (gi + 1)],
                                      psF[gi % 2][:]).then_inc(sem["ey"], 1)
                    v.wait_ge(sem["ey"], 8 * (i + 1))
                    if i >= 2:
                        v.wait_ge(sem["srt"], i - 1)
                    vbar(v, v.reduce_sum(stat[b][:, 0:1], y_tok[:], axis=AX.X))
                    v.wait_ge(sem["sq"], i + 1)
                    vbar(v, v.reduce_sum(stat[b][:, 1:2], xsq[:], axis=AX.X))
                    vbar(v, v.tensor_scalar_mul(stat[b][:, 2:3], stat[b][:, 0:1], 1.0 / D))
                    vbar(v, v.tensor_scalar(out=stat[b][:, 7:8], in0=stat[b][:, 2:3],
                                            scalar1=stat[b][:, 2:3], scalar2=-1e-5,
                                            op0=ALU.mult, op1=ALU.add))
                    v.scalar_tensor_tensor(out=stat[b][:, 3:4], in0=stat[b][:, 1:2],
                                           scalar=1.0 / D, in1=stat[b][:, 7:8],
                                           op0=ALU.mult, op1=ALU.subtract).then_inc(sem["var"], 1)
                    v.wait_ge(sem["srt"], i + 1)
                    vbar(v, v.scalar_tensor_tensor(out=stat[b][:, 6:7], in0=stat[b][:, 2:3],
                                                   scalar=-1.0, in1=stat[b][:, 5:6],
                                                   op0=ALU.mult, op1=ALU.mult))
                    v.tensor_scalar(out=yn_tok[:], in0=y_tok[:],
                                    scalar1=stat[b][:, 5:6], scalar2=stat[b][:, 6:7],
                                    op0=ALU.mult, op1=ALU.add).then_inc(sem["xnr"], 1)

            @block.scalar
            def _(sc):
                ctx.gate(sc)
                for i in range(NT):
                    b = i % 2
                    sc.wait_ge(sem["ey"], i * 8 + 8)
                    if i >= 1:
                        sc.wait_ge(sem["var"], i)
                    sc.activation(out=xsq[:], in_=y_tok[:], func=AF.Square).then_inc(sem["sq"], 1)
                    sc.wait_ge(sem["var"], i + 1)
                    sc.wait_ge(sem["idn"], 3)
                    sbar(sc, sc.activation(out=stat[b][:, 4:5], in_=stat[b][:, 3:4], func=AF.Ln))
                    sc.activation(out=stat[b][:, 5:6], in_=stat[b][:, 4:5], func=AF.Exp,
                                  scale=cpool[:, 1:2]).then_inc(sem["srt"], 1)
                    for gi in range(4):
                        sc.wait_ge(sem["tb"], i * 4 + gi + 1)
                        sc.copy(yn_sb[:, bass.ts(gi, 8), bass.ts(i, 128)],
                                psB[gi % 2][:]).then_inc(sem["eb"], 1)

            @block.sync
            def _(s):
                ctx.gate(s)
                s.wait_ge(sem["eb"], NT * 4)
                s.dma_start(out=yn.rearrange("(ch p) t -> p ch t", p=128),
                            in_=yn_sb[:]).then_inc(sem["yno"], 16)

        with contextlib.ExitStack() as st1:
            wbuf = [st1.enter_context(nc.sbuf_tensor(PFX + f"wbuf{i}", [128, 32, 128], BF16)) for i in range(2)]
            qstage = [st1.enter_context(nc.sbuf_tensor(PFX + f"qstage{i}", [128, T], BF16)) for i in range(2)]
            bq_sb = st1.enter_context(nc.sbuf_tensor(PFX + "bq_sb", [128, 48], F32))
            psQ = [st1.enter_context(nc.psum_tensor(PFX + f"psQ{i}", [128, 4, 512], F32)) for i in range(2)]

            jobs = ([(o, q1) for o in range(16)] + [(o, k1) for o in range(16)]
                    + [(o, v1) for o in range(16)])

            @block.sync
            def _(s):
                ctx.gate(s)
                s.wait_ge(sem["yno"], 16)
                s.dma_start(out=bq_sb[:], in_=bqkv).then_inc(sem["bvl"], 16)

            @block.gpsimd
            def _(g):
                ctx.gate(g)
                g.wait_ge(sem["yno"], 16)
                for j, (o, dest) in enumerate(jobs):
                    if j >= 2:
                        g.wait_ge(sem["mmq"], j - 1)
                    g.dma_start(out=wbuf[j % 2][:],
                                in_=wqkvT.rearrange("(ch p) o -> p ch o", p=128)[:, :, bass.ts(j, 128)]
                                ).then_inc(sem[f"wld{j % 2}"], 16)

            @block.tensor
            def _(t):
                ctx.gate(t)
                t.wait_ge(sem["bvl"], 16)
                for j, (o, dest) in enumerate(jobs):
                    t.wait_ge(sem[f"wld{j % 2}"], 16 * (j // 2 + 1))
                    if j >= 2:
                        t.wait_ge(sem["evq"], (j - 1) * 4)
                    for tc in range(4):
                        for ch in range(32):
                            mi = t.matmul(psQ[j % 2][:, tc, :], wbuf[j % 2][:, ch, :],
                                          yn_sb[:, ch, bass.ts(tc, 512)],
                                          start=(ch == 0), stop=(ch == 31))
                    mi.then_inc(sem["mmq"], 1)

            @block.vector
            def _(v):
                ctx.gate(v)
                for j, (o, dest) in enumerate(jobs):
                    v.wait_ge(sem["mmq"], j + 1)
                    if j >= 2:
                        v.wait_ge(sem[f"qout{j % 2}"], 16 * (j // 2))
                    for tc in range(4):
                        v.tensor_scalar_add(qstage[j % 2][:, bass.ts(tc, 512)],
                                            psQ[j % 2][:, tc, :],
                                            bq_sb[:, j:j + 1]).then_inc(sem["evq"], 1)

            @block.sync
            def _(s):
                ctx.gate(s)
                for j, (o, dest) in enumerate(jobs):
                    s.wait_ge(sem["evq"], (j + 1) * 4)
                    s.dma_start(out=dest[bass.ts(o, 128), :],
                                in_=qstage[j % 2][:]).then_inc(sem[f"qout{j % 2}"], 16)
                s.wait_ge(sem["qout0"], 16 * 24)
                s.wait_ge(sem["qout1"], 16 * 24)




    return sem


def build_l1_pre():
    return _mk_standalone("l1_pre", [("yA", [D, T], F32, "ExternalInput"), ("wqkvT", [D, 6144], F32, "ExternalInput"),
      ("bqkv", [128, 48], F32, "ExternalInput"), ("yn", [D, T], BF16, "ExternalOutput"),
      ("q1", [2048, T], BF16, "ExternalOutput"), ("k1", [2048, T], BF16, "ExternalOutput"),
      ("v1", [2048, T], BF16, "ExternalOutput")], emit_l1_pre)


def emit_l1_attn(ctx, io, NH=8):
    nc = ctx.nc
    block = ctx.block
    PFX = "l1_attn:"
    q1 = io["q1"]
    k1 = io["k1"]
    v1 = io["v1"]
    ctx1 = io["ctx1"]
    pr = io["pr"]
    with contextlib.ExitStack() as top:
        K_sb = top.enter_context(nc.sbuf_tensor(PFX + "K_sb", [128, 16, T], BF16))
        Vt_sb = top.enter_context(nc.sbuf_tensor(PFX + "Vt_sb", [128, 8, 16, 256], BF16))
        qh_sb = [top.enter_context(nc.sbuf_tensor(PFX + f"qh{i}", [128, 2, T], BF16)) for i in range(2)]
        probs_sb = [top.enter_context(nc.sbuf_tensor(PFX + f"pb{i}", [128, T], BF16)) for i in range(2)]
        pT_sb = [top.enter_context(nc.sbuf_tensor(PFX + f"pT{i}", [128, T], BF16)) for i in range(2)]
        ctx_st = top.enter_context(nc.sbuf_tensor(PFX + "ctx_st", [128, 2, T], BF16))
        mask_sb = top.enter_context(nc.sbuf_tensor(PFX + "mask_sb", [128, 128], F32))
        mx = [top.enter_context(nc.sbuf_tensor(PFX + f"mx{i}", [128, 2], F32)) for i in range(2)]
        sums = [top.enter_context(nc.sbuf_tensor(PFX + f"sums{i}", [128, 2], F32)) for i in range(2)]
        ps = [top.enter_context(nc.psum_tensor(PFX + f"ps{i}", [128, 4, 512], F32)) for i in range(2)]
        sem = {n: nc.alloc_semaphore(name="l1_attn:" + n) for n in
               ["k", "vt", "qh0", "qh1", "sc", "msk", "mxs", "ex", "nm",
                "pw0", "pw1", "pt0", "pt1", "cx", "cev", "cout", "idn", "vss"]}
        vss = [0]

        def vbar(v, inst):
            vss[0] += 1
            inst.then_inc(sem["vss"], 1)
            v.wait_ge(sem["vss"], vss[0])

        @block.gpsimd
        def _(g):
            ctx.gate(g)
            g.memset(mask_sb[:], 0.0).then_inc(sem["idn"], 1)
            g.wait_ge(sem["idn"], 1)
            g.affine_select(out=mask_sb[:], in_=mask_sb[:], compare_op=ALU.is_ge,
                            fill=-30000.0, base=0, pattern=[[-1, 128]],
                            channel_multiplier=1).then_inc(sem["idn"], 1)

        @block.sync
        def _(s):
            ctx.gate(s)
            s.dma_start(out=K_sb[:], in_=k1.rearrange("(h p) t -> p h t", p=128)).then_inc(sem["k"], 16)
            for h in range(8):
                for half in range(2):
                    for j in range(16):
                        s.dma_start(out=Vt_sb[:, h, j, bass.ts(half, 128)],
                                    in_=v1[256 * h + 128 * half:256 * h + 128 * (half + 1),
                                           bass.ts(j, 128)],
                                    transpose=True).then_inc(sem["vt"], 16)

        @block.gpsimd
        def _(g):
            ctx.gate(g)
            def ctx_write(hh):
                cv = ctx1.rearrange("(h f p) t -> h f p t", f=2, p=128)
                for half in range(2):
                    g.wait_ge(sem["cev"], 2 * hh + half + 1)
                    g.dma_start(out=cv[hh, half],
                                in_=ctx_st[:, half, :]).then_inc(sem["cout"], 16)

            for h in range(NH):
                if h >= 3:
                    ctx_write(h - 3)
                if h >= 2:
                    g.wait_ge(sem["sc"], 16 * (h - 1))
                g.dma_start(out=qh_sb[h % 2][:],
                            in_=q1.rearrange("(h f p) t -> p h f t", f=2, p=128)[:, h]
                            ).then_inc(sem[f"qh{h % 2}"], 16)
            for hh in range(max(0, NH - 3), NH):
                ctx_write(hh)
            g.wait_ge(sem["cout"], 32 * NH)

        @block.tensor
        def _(t):
            ctx.gate(t)
            t.wait_ge(sem["k"], 16)
            t.wait_ge(sem["vt"], 16 * 256)
            for h in range(NH):
                t.wait_ge(sem[f"qh{h % 2}"], 16 * (h // 2 + 1))
                for qt in range(16):
                    m = 16 * h + qt
                    ext = 128 * (qt + 1)
                    if m >= 2:
                        t.wait_ge(sem["ex"], m - 1)
                    if qt == 1 and h >= 1:
                        t.wait_ge(sem["cev"], 2 * h)
                    nn = (ext + 511) // 512
                    for tc in range(nn):
                        wdt = min(512, ext - 512 * tc)
                        for hf in range(2):
                            mi = t.matmul(ps[m % 2][:, tc, 0:wdt],
                                          qh_sb[h % 2][:, hf, bass.ts(qt, 128)],
                                          K_sb[:, 2 * h + hf, 512 * tc:512 * tc + wdt],
                                          start=(hf == 0), stop=(hf == 1))
                    mi.then_inc(sem["sc"], 1)
                for half in range(2):
                    for kc in range(16):
                        n = 32 * h + 16 * half + kc
                        t.wait_ge(sem[f"pt{n % 2}"], 16 * (n // 2 + 1))
                        for tcq in range(4):
                            mi = t.matmul(ps[1][:, tcq, :],
                                          Vt_sb[:, h, kc, bass.ts(half, 128)],
                                          pT_sb[n % 2][:, bass.ts(tcq, 512)],
                                          start=(kc == 0), stop=(kc == 15))
                        mi.then_inc(sem["cx"], 1)

        @block.vector
        def _(v):
            ctx.gate(v)
            for h in range(NH):
                for qt in range(16):
                    m = 16 * h + qt
                    b = m % 2
                    ext = 128 * (qt + 1)
                    v.wait_ge(sem["sc"], m + 1)
                    v.tensor_add(ps[b].rearrange("p a b -> p (a b)")[:, ext - 128:ext],
                                 ps[b].rearrange("p a b -> p (a b)")[:, ext - 128:ext],
                                 mask_sb[:]).then_inc(sem["msk"], 1)
                    v.wait_ge(sem["msk"], m + 1)
                    vbar(v, v.reduce_max(mx[b][:, 0:1],
                                         ps[b].rearrange("p a b -> p (a b)")[:, 0:ext],
                                         axis=AX.X))
                    v.tensor_scalar_mul(mx[b][:, 1:2], mx[b][:, 0:1], -1.0).then_inc(sem["mxs"], 1)
                    v.wait_ge(sem["ex"], m + 1)
                    vbar(v, v.reciprocal(sums[b][:, 1:2], sums[b][:, 0:1]))
                    if ext < T:
                        v.memset(probs_sb[b][:, ext:T], 0.0)
                    v.tensor_scalar_mul(probs_sb[b][:, 0:ext], probs_sb[b][:, 0:ext],
                                        sums[b][:, 1:2]).then_inc(sem["nm"], 1)

        @block.scalar
        def _(sc):
            ctx.gate(sc)
            for h in range(NH):
                for qt in range(16):
                    m = 16 * h + qt
                    b = m % 2
                    ext = 128 * (qt + 1)
                    sc.wait_ge(sem["mxs"], m + 1)
                    if m >= 2:
                        sc.wait_ge(sem[f"pw{m % 2}"], 16 * (m // 2))
                    sc.activation(out=probs_sb[b][:, 0:ext],
                                  in_=ps[b].rearrange("p a b -> p (a b)")[:, 0:ext],
                                  func=AF.Exp, bias=mx[b][:, 1:2],
                                  accum_out=sums[b][:, 0:1]).then_inc(sem["ex"], 1)
                for half in range(2):
                    for kc in range(16):
                        n = 32 * h + 16 * half + kc
                        sc.wait_ge(sem["pw0"], 16 * 8 * (h + 1))
                        sc.wait_ge(sem["pw1"], 16 * 8 * (h + 1))
                        if n >= 2:
                            sc.wait_ge(sem["cx"], n - 1)
                        sc.dma_start(out=pT_sb[n % 2][:],
                                     in_=pr[h % 2][:, bass.ts(kc, 128)],
                                     transpose=True).then_inc(sem[f"pt{n % 2}"], 16)
                    sc.wait_ge(sem["cx"], 32 * h + 16 * (half + 1))
                    if h >= 1:
                        sc.wait_ge(sem["cout"], 32 * h)
                    sc.copy(ctx_st[:, half, :],
                            ps[1].rearrange("p a b -> p (a b)")).then_inc(sem["cev"], 1)

        @block.sync
        def _(s):
            ctx.gate(s)
            for h in range(NH):
                for qt in range(16):
                    m = 16 * h + qt
                    ext = 128 * (qt + 1)
                    s.wait_ge(sem["nm"], m + 1)
                    if qt == 0 and h >= 2:
                        s.wait_ge(sem["cx"], 32 * (h - 1))
                    s.dma_start(out=pr[h % 2, bass.ts(qt, 128), :],
                                in_=probs_sb[m % 2][:]).then_inc(sem[f"pw{m % 2}"], 16)




    return sem


def build_l1_attn(NH=8):
    return _mk_standalone("l1_attn", [("q1", [2048, T], BF16, "ExternalInput"), ("k1", [2048, T], BF16, "ExternalInput"),
      ("v1", [2048, T], BF16, "ExternalInput"), ("ctx1", [2048, T], BF16, "ExternalOutput"),
      ("pr", [2, T, T], BF16, "Internal")], emit_l1_attn, extra_args={"NH": NH})


def emit_l1_ffn(ctx, io):
    nc = ctx.nc
    block = ctx.block
    PFX = "l1_ffn:"
    yn = io["yn"]
    U1 = io["U1"]
    V1 = io["V1"]
    b1 = io["b1"]
    U2 = io["U2"]
    z1 = io["z1"]
    act1 = io["act1"]
    with contextlib.ExitStack() as top:
        P_sb = top.enter_context(nc.sbuf_tensor(PFX + "P_sb", [128, 8, T], BF16))
        b1_sb = top.enter_context(nc.sbuf_tensor(PFX + "b1_sb", [128, 172], F32))
        sem = {n: nc.alloc_semaphore(name="l1_ffn:" + n) for n in
               ["ynl", "b1l", "u1l0", "u1l1", "mmp", "pev",
                "v1g0", "v1g1", "v1u0", "v1u1", "mmg", "mmu", "sil", "aml",
                "aout0", "aout1", "atl", "u2l0", "u2l1", "mmz", "zev",
                "zout0", "zout1", "vss"]}
        vss = [0]

        with contextlib.ExitStack() as stA:
            yn_sb = stA.enter_context(nc.sbuf_tensor(PFX + "yn_sb", [128, 32, T], BF16))
            u1buf = [stA.enter_context(nc.sbuf_tensor(PFX + f"u1b{i}", [128, 32, 128], BF16)) for i in range(2)]
            psP = [stA.enter_context(nc.psum_tensor(PFX + f"psP{i}", [128, 4, 512], F32)) for i in range(2)]

            @block.sync
            def _(s):
                ctx.gate(s)
                s.dma_start(out=yn_sb[:], in_=yn.rearrange("(ch p) t -> p ch t", p=128)).then_inc(sem["ynl"], 16)
                s.dma_start(out=b1_sb[:], in_=b1).then_inc(sem["b1l"], 16)

            @block.gpsimd
            def _(g):
                ctx.gate(g)
                for rs in range(8):
                    if rs >= 2:
                        g.wait_ge(sem["mmp"], rs - 1)
                    g.dma_start(out=u1buf[rs % 2][:],
                                in_=U1.rearrange("(ch p) r -> p ch r", p=128)[:, :, bass.ts(rs, 128)]
                                ).then_inc(sem[f"u1l{rs % 2}"], 16)

            @block.tensor
            def _(t):
                ctx.gate(t)
                t.wait_ge(sem["ynl"], 16)
                for rs in range(8):
                    t.wait_ge(sem[f"u1l{rs % 2}"], 16 * (rs // 2 + 1))
                    if rs >= 2:
                        t.wait_ge(sem["pev"], (rs - 1) * 4)
                    for tc in range(4):
                        for ch in range(32):
                            mi = t.matmul(psP[rs % 2][:, tc, :], u1buf[rs % 2][:, ch, :],
                                          yn_sb[:, ch, bass.ts(tc, 512)],
                                          start=(ch == 0), stop=(ch == 31))
                    mi.then_inc(sem["mmp"], 1)

            @block.vector
            def _(v):
                ctx.gate(v)
                for rs in range(8):
                    v.wait_ge(sem["mmp"], rs + 1)
                    for tc in range(4):
                        v.tensor_copy(P_sb[:, rs, bass.ts(tc, 512)],
                                      psP[rs % 2][:, tc, :]).then_inc(sem["pev"], 1)

        with contextlib.ExitStack() as stB:
            v1b = {m: [stB.enter_context(nc.sbuf_tensor(PFX + f"v1b_{m}{i}", [128, 8, 128], BF16))
                       for i in range(2)] for m in "gu"}
            sg_sb = stB.enter_context(nc.sbuf_tensor(PFX + "sg_sb", [128, 512], BF16))
            a_st = [stB.enter_context(nc.sbuf_tensor(PFX + f"a_st{i}", [128, 2, 512], BF16)) for i in range(2)]
            gps = [stB.enter_context(nc.psum_tensor(PFX + f"gps{i}", [128, 512], F32)) for i in range(2)]
            ups = [stB.enter_context(nc.psum_tensor(PFX + f"ups{i}", [128, 512], F32)) for i in range(2)]

            NJ = 86
            v1v = V1.rearrange("(ch p) f -> p ch f", p=128)

            @block.gpsimd
            def _(g):
                ctx.gate(g)
                g.wait_ge(sem["pev"], 32)
                for j in range(NJ):
                    for m, base, sname in (("g", 0, "v1g"), ("u", 11008, "v1u")):
                        if j >= 2:
                            g.wait_ge(sem["mmu"], 4 * (j - 1))
                        g.dma_start(out=v1b[m][j % 2][:],
                                    in_=v1v[:, :, base + 128 * j:base + 128 * (j + 1)]
                                    ).then_inc(sem[f"{sname}{j % 2}"], 16)

            @block.tensor
            def _(t):
                ctx.gate(t)
                for j in range(NJ):
                    t.wait_ge(sem[f"v1g{j % 2}"], 16 * (j // 2 + 1))
                    t.wait_ge(sem[f"v1u{j % 2}"], 16 * (j // 2 + 1))
                    for tc in range(4):
                        u = 4 * j + tc
                        if u >= 2:
                            t.wait_ge(sem["sil"], u - 1)
                            t.wait_ge(sem["aml"], u - 1)
                        for ch in range(8):
                            mi = t.matmul(gps[tc % 2][:], v1b["g"][j % 2][:, ch, :],
                                          P_sb[:, ch, bass.ts(tc, 512)],
                                          start=(ch == 0), stop=(ch == 7))
                        mi.then_inc(sem["mmg"], 1)
                        for ch in range(8):
                            mi = t.matmul(ups[tc % 2][:], v1b["u"][j % 2][:, ch, :],
                                          P_sb[:, ch, bass.ts(tc, 512)],
                                          start=(ch == 0), stop=(ch == 7))
                        mi.then_inc(sem["mmu"], 1)

            @block.scalar
            def _(sc):
                ctx.gate(sc)
                sc.wait_ge(sem["b1l"], 16)
                for j in range(NJ):
                    for tc in range(4):
                        u = 4 * j + tc
                        sc.wait_ge(sem["mmg"], u + 1)
                        if u >= 1:
                            sc.wait_ge(sem["aml"], u)
                        sc.activation(out=sg_sb[:], in_=gps[tc % 2][:], func=AF.Silu,
                                      bias=b1_sb[:, j:j + 1]).then_inc(sem["sil"], 1)

            @block.vector
            def _(v):
                ctx.gate(v)
                for j in range(NJ):
                    for tc in range(4):
                        u = 4 * j + tc
                        v.wait_ge(sem["sil"], u + 1)
                        v.wait_ge(sem["mmu"], u + 1)
                        if tc == 0 and j >= 2:
                            v.wait_ge(sem[f"aout{j % 2}"], 32 * (j // 2))
                        if tc == 2:
                            v.wait_ge(sem[f"aout{j % 2}"], 32 * (j // 2) + 16)
                        v.scalar_tensor_tensor(out=a_st[j % 2][:, tc % 2, :],
                                               in0=ups[tc % 2][:],
                                               scalar=b1_sb[:, 86 + j:87 + j],
                                               in1=sg_sb[:], op0=ALU.add,
                                               op1=ALU.mult).then_inc(sem["aml"], 1)

            @block.sync
            def _(s):
                ctx.gate(s)
                av = act1.rearrange("n (th t2) -> n th t2", t2=1024)
                for j in range(NJ):
                    for half in range(2):
                        s.wait_ge(sem["aml"], 4 * j + 2 * (half + 1))
                        s.dma_start(out=av[bass.ts(j, 128), half, :],
                                    in_=a_st[j % 2][:, :, :]).then_inc(sem[f"aout{j % 2}"], 16)

        with contextlib.ExitStack() as stC:
            act_tc = stC.enter_context(nc.sbuf_tensor(PFX + "act_tc", [128, 86, 512], BF16))
            u2buf = [stC.enter_context(nc.sbuf_tensor(PFX + f"u2b{i}", [128, 86, 128], BF16)) for i in range(2)]
            zst = [stC.enter_context(nc.sbuf_tensor(PFX + f"zst{i}", [128, 512], BF16)) for i in range(2)]
            psZ = [stC.enter_context(nc.psum_tensor(PFX + f"psZz{i}", [128, 512], F32)) for i in range(2)]

            @block.gpsimd
            def _(g):
                ctx.gate(g)
                g.wait_ge(sem["aout0"], 32 * 43)
                g.wait_ge(sem["aout1"], 32 * 43)
                for tc in range(4):
                    if tc >= 1:
                        g.wait_ge(sem["mmz"], 8 * tc)
                    g.dma_start(out=act_tc[:],
                                in_=act1.rearrange("(g p) t -> p g t", p=128)[:, :, bass.ts(tc, 512)]
                                ).then_inc(sem["atl"], 16)
                    for rs in range(8):
                        j = 8 * tc + rs
                        if j >= 2:
                            g.wait_ge(sem["mmz"], j - 1)
                        g.dma_start(out=u2buf[rs % 2][:],
                                    in_=U2.rearrange("(g p) r -> p g r", p=128)[:, :, bass.ts(rs, 128)]
                                    ).then_inc(sem[f"u2l{rs % 2}"], 16)

            @block.tensor
            def _(t):
                ctx.gate(t)
                for tc in range(4):
                    t.wait_ge(sem["atl"], 16 * (tc + 1))
                    for rs in range(8):
                        j = 8 * tc + rs
                        t.wait_ge(sem[f"u2l{rs % 2}"], 16 * (j // 2 + 1))
                        if j >= 2:
                            t.wait_ge(sem["zev"], j - 1)
                        for gg in range(86):
                            mi = t.matmul(psZ[j % 2][:], u2buf[rs % 2][:, gg, :],
                                          act_tc[:, gg, :], start=(gg == 0),
                                          stop=(gg == 85))
                        mi.then_inc(sem["mmz"], 1)

            @block.vector
            def _(v):
                ctx.gate(v)
                for tc in range(4):
                    for rs in range(8):
                        j = 8 * tc + rs
                        v.wait_ge(sem["mmz"], j + 1)
                        if j >= 2:
                            v.wait_ge(sem[f"zout{j % 2}"], 16 * (j // 2))
                        v.tensor_copy(zst[j % 2][:], psZ[j % 2][:]).then_inc(sem["zev"], 1)

            @block.sync
            def _(s):
                ctx.gate(s)
                for tc in range(4):
                    for rs in range(8):
                        j = 8 * tc + rs
                        s.wait_ge(sem["zev"], j + 1)
                        s.dma_start(out=z1[bass.ts(rs, 128), bass.ts(tc, 512)],
                                    in_=zst[j % 2][:]).then_inc(sem[f"zout{j % 2}"], 16)
                s.wait_ge(sem["zout0"], 16 * 16)
                s.wait_ge(sem["zout1"], 16 * 16)




    return sem


def build_l1_ffn():
    return _mk_standalone("l1_ffn", [("yn", [D, T], BF16, "ExternalInput"), ("U1", [D, 1024], F32, "ExternalInput"),
      ("V1", [1024, 22016], F32, "ExternalInput"), ("b1", [128, 172], F32, "ExternalInput"),
      ("U2", [11008, 1024], F32, "ExternalInput"), ("z1", [1024, T], BF16, "ExternalOutput"),
      ("act1", [11008, T], BF16, "Internal")], emit_l1_ffn)


def emit_l1_out(ctx, io):
    nc = ctx.nc
    block = ctx.block
    PFX = "l1_out:"
    z1 = io["z1"]
    ctx1 = io["ctx1"]
    V2 = io["V2"]
    woT = io["woT"]
    badd2 = io["badd2"]
    yA = io["yA"]
    out = io["out"]
    with contextlib.ExitStack() as top:
        z_sb = top.enter_context(nc.sbuf_tensor(PFX + "z_sb", [128, 8, T], BF16))
        ctx_sb = top.enter_context(nc.sbuf_tensor(PFX + "ctx_sb", [128, 16, T], BF16))
        identF = top.enter_context(nc.sbuf_tensor(PFX + "identF", [128, 128], F32))
        badd_sb = top.enter_context(nc.sbuf_tensor(PFX + "badd_sb", [128, 32, 2], F32))
        badd = top.enter_context(nc.sbuf_tensor(PFX + "badd", [128, 32], F32))
        v2buf = [top.enter_context(nc.sbuf_tensor(PFX + f"v2b{i}", [128, 8, 128], BF16)) for i in range(2)]
        wobuf = [top.enter_context(nc.sbuf_tensor(PFX + f"wob{i}", [128, 16, 128], BF16)) for i in range(2)]
        ya_sb = [top.enter_context(nc.sbuf_tensor(PFX + f"ya{i}", [128, 512], F32)) for i in range(2)]
        y2_sb = [top.enter_context(nc.sbuf_tensor(PFX + f"y2{i}", [128, 512], F32)) for i in range(2)]
        ostage = [top.enter_context(nc.sbuf_tensor(PFX + f"ost{i}", [128, 4, 128], F32)) for i in range(2)]
        psY = [top.enter_context(nc.psum_tensor(PFX + f"psY{i}", [128, 512], F32)) for i in range(2)]
        psT = [top.enter_context(nc.psum_tensor(PFX + f"psT{i}", [128, 4, 128], F32)) for i in range(2)]
        sem = {n: nc.alloc_semaphore(name="l1_out:" + n) for n in
               ["zl", "cl", "bl", "idn", "v2l0", "v2l1", "wol0", "wol1",
                "yal0", "yal1", "mm", "yev", "ttr", "tev", "oout0", "oout1", "vss"]}
        vss = [0]

        def vbar(v, inst):
            vss[0] += 1
            inst.then_inc(sem["vss"], 1)
            v.wait_ge(sem["vss"], vss[0])

        @block.gpsimd
        def _(g):
            ctx.gate(g)
            g.memset(identF[:], 1.0).then_inc(sem["idn"], 1)
            g.wait_ge(sem["idn"], 1)
            g.affine_select(out=identF[:], in_=identF[:], compare_op=ALU.is_equal,
                            fill=0.0, base=0, pattern=[[-1, 128]],
                            channel_multiplier=1).then_inc(sem["idn"], 1)

        @block.sync
        def _(s):
            ctx.gate(s)
            s.dma_start(out=z_sb[:], in_=z1.rearrange("(ch p) t -> p ch t", p=128)).then_inc(sem["zl"], 16)
            s.dma_start(out=ctx_sb[:], in_=ctx1.rearrange("(ch p) t -> p ch t", p=128)).then_inc(sem["cl"], 16)
            s.dma_start(out=badd_sb[:], in_=badd2).then_inc(sem["bl"], 16)

        @block.vector
        def _(v):
            ctx.gate(v)
            v.wait_ge(sem["bl"], 16)
            vbar(v, v.tensor_add(badd[:], badd_sb[:, :, 0], badd_sb[:, :, 1]))
            for tc in range(4):
                for os in range(32):
                    j = 32 * tc + os
                    b = j % 2
                    v.wait_ge(sem["mm"], j + 1)
                    v.wait_ge(sem[f"yal{b}"], 16 * (j // 2 + 1))
                    if j >= 2:
                        v.wait_ge(sem["ttr"], j - 1)
                    v.scalar_tensor_tensor(out=y2_sb[b][:], in0=psY[b][:],
                                           scalar=badd[:, os:os + 1], in1=ya_sb[b][:],
                                           op0=ALU.add, op1=ALU.add).then_inc(sem["yev"], 1)
                    if j >= 1:
                        jj = j - 1
                        v.wait_ge(sem["ttr"], jj + 1)
                        if jj >= 2:
                            v.wait_ge(sem[f"oout{jj % 2}"], 16 * (jj // 2))
                        v.tensor_copy(ostage[jj % 2][:], psT[jj % 2][:]).then_inc(sem["tev"], 1)
            jj = 127
            v.wait_ge(sem["ttr"], 128)
            v.wait_ge(sem[f"oout{jj % 2}"], 16 * (jj // 2))
            v.tensor_copy(ostage[jj % 2][:], psT[jj % 2][:]).then_inc(sem["tev"], 1)

        @block.gpsimd
        def _(g):
            ctx.gate(g)
            for tc in range(4):
                for os in range(32):
                    j = 32 * tc + os
                    b = j % 2
                    if j >= 2:
                        g.wait_ge(sem["mm"], j - 1)
                        g.wait_ge(sem["yev"], j - 1)
                    g.dma_start(out=v2buf[b][:],
                                in_=V2.rearrange("(ch p) o -> p ch o", p=128)[:, :, bass.ts(os, 128)]
                                ).then_inc(sem[f"v2l{b}"], 16)
                    g.dma_start(out=wobuf[b][:],
                                in_=woT.rearrange("(ch p) o -> p ch o", p=128)[:, :, bass.ts(os, 128)]
                                ).then_inc(sem[f"wol{b}"], 16)
                    g.dma_start(out=ya_sb[b][:],
                                in_=yA[bass.ts(os, 128), bass.ts(tc, 512)]).then_inc(sem[f"yal{b}"], 16)

        @block.tensor
        def _(t):
            ctx.gate(t)
            t.wait_ge(sem["zl"], 16)
            t.wait_ge(sem["cl"], 16)
            t.wait_ge(sem["idn"], 2)
            for tc in range(4):
                for os in range(32):
                    j = 32 * tc + os
                    b = j % 2
                    t.wait_ge(sem[f"v2l{b}"], 16 * (j // 2 + 1))
                    t.wait_ge(sem[f"wol{b}"], 16 * (j // 2 + 1))
                    if j >= 2:
                        t.wait_ge(sem["yev"], j - 1)
                    for ch in range(8):
                        t.matmul(psY[b][:], v2buf[b][:, ch, :],
                                 z_sb[:, ch, bass.ts(tc, 512)],
                                 start=(ch == 0), stop=False)
                    for ch in range(16):
                        mi = t.matmul(psY[b][:], wobuf[b][:, ch, :],
                                      ctx_sb[:, ch, bass.ts(tc, 512)],
                                      start=False, stop=(ch == 15))
                    mi.then_inc(sem["mm"], 1)
                    if j >= 1:
                        jj = j - 1
                        t.wait_ge(sem["yev"], jj + 1)
                        if jj >= 2:
                            t.wait_ge(sem["tev"], jj - 1)
                        for sub in range(4):
                            ti = t.transpose(psT[jj % 2][:, sub, :],
                                             y2_sb[jj % 2][:, bass.ts(sub, 128)],
                                             identF[:])
                        ti.then_inc(sem["ttr"], 1)
            jj = 127
            t.wait_ge(sem["yev"], 128)
            t.wait_ge(sem["tev"], 126)
            for sub in range(4):
                ti = t.transpose(psT[jj % 2][:, sub, :],
                                 y2_sb[jj % 2][:, bass.ts(sub, 128)], identF[:])
            ti.then_inc(sem["ttr"], 1)

        @block.sync
        def _(s):
            ctx.gate(s)
            ov = out.rearrange("(a j p) o -> p a j o", j=4, p=128)
            for tc in range(4):
                for os in range(32):
                    j = 32 * tc + os
                    s.wait_ge(sem["tev"], j + 1)
                    s.dma_start(out=ov[:, tc, :, bass.ts(os, 128)],
                                in_=ostage[j % 2][:]).then_inc(sem[f"oout{j % 2}"], 16)
            s.wait_ge(sem["oout0"], 16 * 64)
            s.wait_ge(sem["oout1"], 16 * 64)

    return sem


def build_l1_out():
    return _mk_standalone("l1_out", [("z1", [1024, T], BF16, "ExternalInput"), ("ctx1", [2048, T], BF16, "ExternalInput"),
      ("V2", [1024, D], F32, "ExternalInput"), ("woT", [2048, D], F32, "ExternalInput"),
      ("badd2", [128, 32, 2], F32, "ExternalInput"), ("yA", [D, T], F32, "ExternalInput"),
      ("out", [T, D], F32, "ExternalOutput")], emit_l1_out)


FENCES = {
    "l0_pre": [("mmq", 48), ("evq", 192), ("eb", 64), ("wld0", 384), ("wld1", 384),
               ("qout0", 384), ("qout1", 384), ("xno", 16), ("xto", 256)],
    "l0_attn": [("cx", 512), ("nm", 512), ("cev", 32), ("pw0", 4096), ("pw1", 4096),
                ("cout", 512), ("pt0", 4096), ("pt1", 4096)],
    "l0_ffn1": [("mmu", 352), ("hml", 352), ("sil", 352), ("hout0", 1408),
                ("hout1", 1408), ("zl", 704)],
    "l0_out": [("mm", 128), ("yev", 128), ("sge", 344), ("yout0", 1024),
               ("yout1", 1024), ("xtl0", 1024), ("xtl1", 1024)],
    "l1_pre": [("mmq", 48), ("evq", 192), ("eb", 64), ("wld0", 384), ("wld1", 384),
               ("qout0", 384), ("qout1", 384), ("yno", 16)],
    "l1_attn": [("cx", 256), ("nm", 128), ("cev", 16), ("cout", 256), ("pw0", 1024),
                ("pw1", 1024), ("pt0", 2048), ("pt1", 2048)],
    "l1_ffn": [("mmz", 32), ("zev", 32), ("sil", 344), ("zout0", 256), ("zout1", 256),
               ("aout0", 1376), ("aout1", 1376), ("u2l0", 256), ("u2l1", 256)],
}


def build_fused(n_phases=8):
    nc = bass.Bass("TRN2", target_bir_lowering=False)
    io = {}
    ins = [("x", [T, D], F32), ("wqT", [D, D], F32), ("wkT", [D, 1024], F32),
           ("wvT", [D, 1024], F32), ("bv", [128, 8], F32),
           ("gqT", [1408, D], I32), ("uqT", [1408, D], I32),
           ("Sg", [11, 128, 8, 32], F32), ("Su", [11, 128, 8, 32], F32),
           ("zgT", [1408, 32], I32), ("zuT", [1408, 32], I32),
           ("dqT", [512, 11008], I32), ("Sd", [4, 128, 8, 86], F32),
           ("zdT", [512, 86], I32), ("woTp", [D, 4, 8, 128], F32),
           ("wqkvT", [D, 6144], F32), ("bqkv", [128, 48], F32),
           ("U1", [D, 1024], F32), ("V1", [1024, 22016], F32),
           ("b1", [128, 172], F32), ("U2", [11008, 1024], F32),
           ("V2", [1024, D], F32), ("woT", [2048, D], F32),
           ("badd2", [128, 32, 2], F32)]
    for nm, shape, dt in ins:
        io[nm] = nc.dram_tensor(nm, shape, dt, kind="ExternalInput").ap()
    io["out"] = nc.dram_tensor("out", [T, D], F32, kind="ExternalOutput").ap()
    scratch = [("xT", [D, T], F32), ("xn", [D, T], BF16), ("q0", [D, T], BF16),
               ("k0", [1024, T], BF16), ("v0", [1024, T], BF16),
               ("ctx0", [D, T], BF16), ("pr", [2, T, T], BF16),
               ("h0", [11008, T], BF16), ("sigd3", [32, T], F32),
               ("yA", [D, T], F32), ("wdq", [4, 8, 86, 128, 128], BF16),
               ("sigd4", [86, T], F32), ("yn", [D, T], BF16),
               ("q1", [2048, T], BF16), ("k1", [2048, T], BF16),
               ("v1", [2048, T], BF16), ("ctx1", [2048, T], BF16),
               ("z1", [1024, T], BF16), ("act1", [11008, T], BF16)]
    for nm, shape, dt in scratch:
        io[nm] = nc.dram_tensor(nm, shape, dt, kind="Internal").ap()

    def sub(**ren):
        d = dict(io)
        for k, v in ren.items():
            d[k] = io[v]
        return d

    with contextlib.ExitStack() as g:
        block = g.enter_context(nc.Block())
        ctx = FusedCtx(nc, block, g)
        bnd = nc.alloc_semaphore(name="bnd")
        seq = [(emit_l0_pre, io, "l0_pre"),
               (emit_l0_attn, io, "l0_attn"),
               (emit_l0_ffn1, sub(sigd="sigd3"), "l0_ffn1"),
               (emit_l0_out, sub(sigd="sigd4"), "l0_out"),
               (emit_l1_pre, io, "l1_pre"),
               (emit_l1_attn, io, "l1_attn"),
               (emit_l1_ffn, io, "l1_ffn"),
               (emit_l1_out, io, "l1_out")]
        seq = seq[:n_phases]
        for i, (emit, iod, name) in enumerate(seq):
            sems = emit(ctx, iod)
            if i < len(seq) - 1:
                fence = FENCES[name]

                @block.gpsimd
                def _(gp, fence=fence, sems=sems):
                    for n, v in fence:
                        gp.wait_ge(sems[n], v)
                    gp.engine_nop().then_inc(bnd, 1)
                ctx.fence = [(bnd, i + 1)]
    return nc



def _t(a):
    return np.ascontiguousarray(np.asarray(a).T)


def _prep_awq_gu(qw, scales, qz):
    qw = np.asarray(qw, dtype=np.int32)
    qz = np.asarray(qz, dtype=np.int32)
    sc = np.asarray(scales, dtype=np.float32)
    qT = np.zeros((1408, qw.shape[0]), np.int32)
    qT[:1376] = qw.T
    zT = np.zeros((1408, 32), np.int32)
    zT[:1376] = qz.T
    scv = sc.reshape(32, 1376, 8)
    S = np.zeros((11, 128, 8, 32), np.float32)
    full = scv.transpose(1, 2, 0)
    S.reshape(1408, 8, 32)[:1376] = full
    return qT, np.ascontiguousarray(S), zT


def host_prep(inp):
    d = {}
    d["x"] = np.asarray(inp["L0_attn_hidden_states"], dtype=np.float32)
    d["wqT"] = _t(inp["L0_attn_Wq"])
    d["wkT"] = _t(inp["L0_attn_Wk"])
    d["wvT"] = _t(inp["L0_attn_Wv"])
    d["bv"] = np.ascontiguousarray(np.asarray(inp["L0_attn_bv"], dtype=np.float32).reshape(8, 128).T)
    d["gqT"], d["Sg"], d["zgT"] = _prep_awq_gu(
        inp["L0_ffn_gate_qweight"], inp["L0_ffn_gate_scales"], inp["L0_ffn_gate_qzeros"])
    d["uqT"], d["Su"], d["zuT"] = _prep_awq_gu(
        inp["L0_ffn_up_qweight"], inp["L0_ffn_up_scales"], inp["L0_ffn_up_qzeros"])
    d["dqT"] = np.ascontiguousarray(np.asarray(inp["L0_ffn_down_qweight"], np.int32).T)
    d["zdT"] = np.ascontiguousarray(np.asarray(inp["L0_ffn_down_qzeros"], np.int32).T)
    scd = np.asarray(inp["L0_ffn_down_scales"], np.float32).reshape(86, 512, 8)
    d["Sd"] = np.ascontiguousarray(scd.transpose(1, 2, 0).reshape(4, 128, 8, 86))
    woT = _t(inp["L0_attn_Wo"])
    d["woTp"] = np.ascontiguousarray(
        woT.reshape(4096, 4, 128, 8).transpose(0, 1, 3, 2))
    d["wqkvT"] = _t(inp["L1_attn_w_qkv"])
    d["bqkv"] = np.ascontiguousarray(np.asarray(inp["L1_attn_b_qkv"], np.float32).reshape(48, 128).T)
    d["U1"] = np.ascontiguousarray(np.asarray(inp["L1_ffn_U1"], np.float32))
    d["V1"] = np.ascontiguousarray(np.asarray(inp["L1_ffn_V1"], np.float32))
    d["b1"] = np.ascontiguousarray(np.asarray(inp["L1_ffn_b1"], np.float32).reshape(172, 128).T)
    d["U2"] = np.ascontiguousarray(np.asarray(inp["L1_ffn_U2"], np.float32))
    d["V2"] = np.ascontiguousarray(np.asarray(inp["L1_ffn_V2"], np.float32))
    d["woT1"] = _t(inp["L1_attn_w_o"])
    b2 = np.asarray(inp["L1_ffn_b2"], np.float32).reshape(32, 128).T
    bo = np.asarray(inp["L1_attn_b_o"], np.float32).reshape(32, 128).T
    d["badd2"] = np.ascontiguousarray(np.stack([b2, bo], axis=-1))
    return d


FUSED = True


def _fused_in_map(d, xi):
    return {"x": np.ascontiguousarray(xi), "wqT": d["wqT"], "wkT": d["wkT"],
            "wvT": d["wvT"], "bv": d["bv"], "gqT": d["gqT"], "uqT": d["uqT"],
            "Sg": d["Sg"], "Su": d["Su"], "zgT": d["zgT"], "zuT": d["zuT"],
            "dqT": d["dqT"], "Sd": d["Sd"], "zdT": d["zdT"], "woTp": d["woTp"],
            "wqkvT": d["wqkvT"], "bqkv": d["bqkv"], "U1": d["U1"], "V1": d["V1"],
            "b1": d["b1"], "U2": d["U2"], "V2": d["V2"], "woT": d["woT1"],
            "badd2": d["badd2"]}


def kernel(**inp):
    d = host_prep(inp)
    x_full = d["x"]
    R = list(range(NCORES))

    if FUSED:
        nc = build_fused()
        res = run_bass_kernel_spmd(nc, [_fused_in_map(d, x_full[i]) for i in R],
                                   core_ids=R).results
        return np.stack([np.asarray(res[i]["out"], np.float32) for i in R], axis=0)

    bfv = ml_dtypes.bfloat16

    nc1 = build_l0_pre()
    r1 = run_bass_kernel_spmd(nc1, [
        {"x": np.ascontiguousarray(x_full[i]), "wqT": d["wqT"], "wkT": d["wkT"],
         "wvT": d["wvT"], "bv": d["bv"]} for i in R], core_ids=R).results

    nc2 = build_l0_attn()
    r2 = run_bass_kernel_spmd(nc2, [
        {"q0": np.asarray(r1[i]["q0"], bfv), "k0": np.asarray(r1[i]["k0"], bfv),
         "v0": np.asarray(r1[i]["v0"], bfv)} for i in R], core_ids=R).results

    nc3 = build_l0_ffn1()
    r3 = run_bass_kernel_spmd(nc3, [
        {"xn": np.asarray(r1[i]["xn"], bfv), "gqT": d["gqT"], "uqT": d["uqT"],
         "Sg": d["Sg"], "Su": d["Su"], "zgT": d["zgT"], "zuT": d["zuT"]}
        for i in R], core_ids=R).results

    nc4 = build_l0_out()
    r4 = run_bass_kernel_spmd(nc4, [
        {"h0": np.asarray(r3[i]["h0"], bfv), "ctx0": np.asarray(r2[i]["ctx0"], bfv),
         "dqT": d["dqT"], "Sd": d["Sd"], "zdT": d["zdT"], "woTp": d["woTp"],
         "xT": np.asarray(r1[i]["xT"], np.float32)} for i in R], core_ids=R).results

    nc5 = build_l1_pre()
    r5 = run_bass_kernel_spmd(nc5, [
        {"yA": np.asarray(r4[i]["yA"], np.float32), "wqkvT": d["wqkvT"],
         "bqkv": d["bqkv"]} for i in R], core_ids=R).results

    nc6 = build_l1_attn()
    r6 = run_bass_kernel_spmd(nc6, [
        {"q1": np.asarray(r5[i]["q1"], bfv), "k1": np.asarray(r5[i]["k1"], bfv),
         "v1": np.asarray(r5[i]["v1"], bfv)} for i in R], core_ids=R).results

    nc7 = build_l1_ffn()
    r7 = run_bass_kernel_spmd(nc7, [
        {"yn": np.asarray(r5[i]["yn"], bfv), "U1": d["U1"], "V1": d["V1"],
         "b1": d["b1"], "U2": d["U2"]} for i in R], core_ids=R).results

    nc8 = build_l1_out()
    r8 = run_bass_kernel_spmd(nc8, [
        {"z1": np.asarray(r7[i]["z1"], bfv), "ctx1": np.asarray(r6[i]["ctx1"], bfv),
         "V2": d["V2"], "woT": d["woT1"], "badd2": d["badd2"],
         "yA": np.asarray(r4[i]["yA"], np.float32)} for i in R], core_ids=R).results

    return np.stack([np.asarray(r8[i]["out"], np.float32) for i in R], axis=0)
